# version 01
# baseline (speedup 1.0000x reference)
import math
import numpy as np
import ml_dtypes
import concourse.bass as bass
import concourse.mybir as mybir
from concourse.bass_utils import run_bass_kernel_spmd

F32 = mybir.dt.float32
BF16 = mybir.dt.bfloat16
U8 = mybir.dt.uint8
AF = mybir.ActivationFunctionType
ALU = mybir.AluOpType
AX = mybir.AxisListType

D = 2048
S = 2048
B = 4
NT = 1024
DTl = 16
FF = 8192
FT = 64
MO = 1408
MOT = 11
E = 16
GAMMA = 0.96875
EPS = 1e-5
INV_SQRT_FF = 1.0 / math.sqrt(FF)
L1_SCALE = 1.0 / math.sqrt(128.0)
KB = 1024

DEBUG = False
UPTO = 99
HAR_MAPS = []
BFLY_STAGES = 6


def _xpos_tables(downscale):
    d = D
    half = d // 2
    sv = (np.arange(0, d, 2, dtype=np.float64) + 0.4 * d) / (1.4 * d)
    power = np.arange(S, dtype=np.float64) / 512.0
    scale = sv[None, :] ** power[:, None]
    if downscale:
        scale = 1.0 / scale
    inv_freq = 1.0 / 10000.0 ** (np.arange(half, dtype=np.float64) / half)
    sinusoid = np.arange(S, dtype=np.float64)[:, None] * inv_freq[None, :]
    sin_s = np.repeat(np.sin(sinusoid) * scale, 2, axis=-1)
    cos_s = np.repeat(np.cos(sinusoid) * scale, 2, axis=-1)
    return cos_s.astype(np.float32), sin_s.astype(np.float32)


def _decay_T():
    n = np.arange(S, dtype=np.int64)
    diff = n[None, :] - n[:, None]
    Dm = np.where(diff >= 0, GAMMA ** np.maximum(diff, 0).astype(np.float64), 0.0)
    return Dm.astype(np.float32)


def _hadamard(n):
    h = np.array([[1.0]])
    while h.shape[0] < n:
        h = np.block([[h, h], [h, -h]])
    return h.astype(np.float32)


def _prot_T():
    M = np.zeros((128, 128), dtype=np.float32)
    j = np.arange(64)
    M[2 * j, 2 * j + 1] = -1.0
    M[2 * j + 1, 2 * j] = 1.0
    return np.ascontiguousarray(M.T)


def build_program(upto=None):
    HAR_MAPS.clear()
    if upto is None:
        upto = UPTO
    nc = bass.Bass("TRN2", target_bir_lowering=False)

    def din(name, shape):
        return nc.dram_tensor(name, shape, F32, kind="ExternalInput").ap()

    x_own = din("x_own", [NT, D])
    x_oth = din("x_oth", [NT, D])
    wq = din("wq", [D, D])
    wk = din("wk", [D, D])
    wv = din("wv", [D, D])
    w1t = din("w1t", [D, FF])
    w3t = din("w3t", [D, FF])
    w2t = din("w2t", [FF, D])
    gwt = din("gwt", [D, E])
    wgt = din("wgt", [E, D, MO])
    wut = din("wut", [E, D, MO])
    wdt = din("wdt", [E, MO, D])
    cosq = din("cosq", [D, NT])
    sinq = din("sinq", [D, NT])
    cosk = din("cosk", [D, S])
    sink = din("sink", [D, S])
    dt_in = din("dt", [S, NT])
    ident_in = din("ident", [128, 128])
    h128_in = din("h128", [128, 128])
    prot_in = din("prot", [128, 128])
    ones_in = din("ones1", [1, 128])
    sel_in = din("sel", [16, D])

    y_out = nc.dram_tensor("y", [NT, D], F32, kind="ExternalOutput").ap()

    sk = "ExternalOutput" if DEBUG else "Internal"
    vspill = nc.dram_tensor("vspill", [S, D], BF16, kind=sk).ap()
    ktspill = nc.dram_tensor("ktspill", [D, S], BF16, kind=sk).ap()
    x1spill = nc.dram_tensor("x1spill", [NT, D], F32, kind=sk).ap()

    dbg = {}
    if DEBUG:
        for name, w, dt in [("dbg_xnT", 2 * DTl * NT, BF16), ("dbg_qt", DTl * NT, BF16),
                            ("dbg_att", DTl * NT, BF16), ("dbg_oacc", DTl * NT, F32),
                            ("dbg_xn1T", DTl * NT, BF16), ("dbg_out1", DTl * NT, F32),
                            ("dbg_cw", 8 * E, F32), ("dbg_hid", FT * 512, BF16)]:
            dbg[name] = nc.dram_tensor(name, [128, w], dt, kind="ExternalOutput").ap()

    sems, cnt = {}, {}
    for s in ["sq", "gq", "aq", "tm", "vm", "sm", "paU", "paF", "pbU", "pbF",
              "ptU", "ptF", "xbU", "xbF", "hbU", "wsF", "evF", "kbU"]:
        sems[s] = nc.semaphore(s).__enter__()
        cnt[s] = 0

    def inc(instr, s, k=1):
        cnt[s] += k
        return instr.then_inc(sems[s], k)

    r0 = nc.sbuf_tensor("r0", [128, 6 * KB], U8).__enter__()
    rs = nc.sbuf_tensor("rs", [128, 40 * KB], U8).__enter__()
    big = nc.sbuf_tensor("big", [128, 160 * KB], U8).__enter__()
    r0f, r0b = r0.bitcast(F32), r0.bitcast(BF16)
    rsf, rsb = rs.bitcast(F32), rs.bitcast(BF16)
    bigf, bigb = big.bitcast(F32), big.bitcast(BF16)
    BIG_FW = 160 * KB // 4
    BIG_BW = 160 * KB // 2
    RS_FW = 40 * KB // 4
    RS_BW = 40 * KB // 2

    def cf(view, off_b, n):
        return view[:, off_b // 4: off_b // 4 + n]

    def cb(view, off_b, n):
        return view[:, off_b // 2: off_b // 2 + n]

    def sapf(off_b, dims):
        return bass.AP(bigf, off_b // 4, [[BIG_FW, 128]] + dims)

    def sapb(off_b, dims):
        return bass.AP(bigb, off_b // 2, [[BIG_BW, 128]] + dims)

    ident_f = cf(r0f, 0, 128)
    prot_f = cf(r0f, 512, 128)
    ones_f = cf(r0f, 1024, 128)[0:1, :]
    eps_ap = cf(r0f, 1536, 1)
    ident_b = cb(r0b, 2560, 128)
    h128_b = cb(r0b, 2816, 128)
    mean_all = cf(r0f, 3072, 16)
    qsum_all = cf(r0f, 3136, 16)
    var_all = cf(r0f, 3200, 16)
    lnv_all = cf(r0f, 3264, 16)
    istd_all = cf(r0f, 3328, 16)
    nbias_all = cf(r0f, 3392, 16)
    nmax2 = cf(r0f, 3456, 2)
    zden2 = cf(r0f, 3520, 2)
    rz2 = cf(r0f, 3584, 2)
    m8t = cf(r0f, 3648, 8)
    gj = cf(r0f, 3712, 16)
    gw_blk = cb(r0b, 4096, DTl * E)

    xstage = [cf(rsf, 0, D), cf(rsf, 8 * KB, D)]
    xnbf = [cb(rsb, 16 * KB, D), cb(rsb, 20 * KB, D)]
    kstage = cf(rsf, 0, D)
    c1stage = cf(rsf, 8 * KB, D)
    s1stage = cf(rsf, 16 * KB, D)
    kbf = cb(rsb, 24 * KB, D)
    t1c = [cf(rsf, 28 * KB, 512), cf(rsf, 30 * KB, 512)]
    t2c = [cf(rsf, 32 * KB, 512), cf(rsf, 34 * KB, 512)]
    kblk = [cb(rsb, 0, D), cb(rsb, 4 * KB, D)]
    dstage = [cf(rsf, 8 * KB, NT), cf(rsf, 12 * KB, NT)]
    vgblk = [cb(rsb, 0, 4096), cb(rsb, 8 * KB, 4096)]
    w1blk = [cb(rsb, 0, D), cb(rsb, 4 * KB, D)]
    w3blk = [cb(rsb, 8 * KB, D), cb(rsb, 12 * KB, D)]
    w2blk = [cb(rsb, 16 * KB, 4096), cb(rsb, 24 * KB, 4096)]
    stmp = [cf(rsf, 32 * KB, 512), cf(rsf, 34 * KB, 512)]
    hbr = [cb(rsb, 36 * KB, 512), cb(rsb, 37 * KB, 512)]
    bfly_sp = [cb(rsb, 24 * KB + q * KB, 512) for q in range(4)]
    x1buf = [cf(rsf, 24 * KB, D), cf(rsf, 32 * KB, D)]
    e_sb = cb(rsb, 0, 512)
    a_sb = cb(rsb, 1 * KB, 512)
    at_sb = cb(rsb, 2 * KB, 512)
    g_sb = cb(rsb, 3 * KB, 256)
    cw_all = cf(rsf, 4 * KB, 8 * E)
    cwT = cb(rsb, 20 * KB, NT)
    bcw = [cb(rsb, 16 * KB, NT), cb(rsb, 18 * KB, NT)]
    wdblk = [cb(rsb, 22 * KB, MOT * 128), cb(rsb, 25 * KB, MOT * 128)]
    stmp2 = [cf(rsf, 28 * KB, NT), cf(rsf, 32 * KB, NT)]
    sel_b = cb(rsb, 36 * KB, D)
    ybuf = [cf(rsf, 16 * KB, D), cf(rsf, 24 * KB, D)]

    def xnT(half, k):
        return cb(bigb, (0 if half == 0 else 32 * KB) + k * (NT * 2), NT)

    def wfull_t(k):
        return cb(bigb, 64 * KB + k * (D * 2), D)

    def qt_t(k):
        return cb(bigb, 128 * KB + k * (NT * 2), NT)

    def att_t(m):
        return cb(bigb, 32 * KB + m * (NT * 2), NT)

    def oacc_t(k):
        return cf(bigf, 96 * KB + k * (NT * 4), NT)

    HAR0 = 32 * KB

    def har_t(j):
        return cb(bigb, HAR0 + j * KB, 512)

    def xn1T(k):
        return cb(bigb, 32 * KB + k * (NT * 2), NT)

    def out1_t(k):
        return cf(bigf, 96 * KB + k * (NT * 4), NT)

    def hidT_t(j):
        if j < 16:
            return cb(bigb, j * (NT * 2), NT)
        return cb(bigb, 64 * KB + (j - 16) * (NT * 2), NT)

    pA = nc.psum_tensor("pA", [128, 2048], F32).__enter__()
    pB = nc.psum_tensor("pB", [128, 1536], F32).__enter__()
    pT = nc.psum_tensor("pT", [128, 1024], BF16).__enter__()
    pB_bf = pB.bitcast(BF16)

    with nc.Block() as block:

        def gate():
            vals = {s: cnt[s] for s in sems if cnt[s] > 0}
            for en in ("sync", "tensor", "vector", "scalar", "gpsimd"):
                @getattr(block, en)
                def _(e, vals=vals):
                    for s, v in vals.items():
                        e.wait_ge(sems[s], v)

        def dbg_dump(pairs):
            if not DEBUG:
                return
            gate()
            @block.scalar
            def _(sc):
                for dst, src in pairs:
                    inc(sc.dma_start(dst, src), "aq", 16)
                sc.wait_ge(sems["aq"], cnt["aq"])
            gate()

        @block.sync
        def _(s):
            inc(s.dma_start(ident_f, ident_in[:, :]), "sq", 16)
            inc(s.dma_start(prot_f, prot_in[:, :]), "sq", 16)
            inc(s.dma_start(ones_f, ones_in[:, :]), "sq", 16)

        @block.gpsimd
        def _(g):
            inc(g.dma_start(ident_b, ident_in[:, :]), "gq", 16)
            inc(g.dma_start(h128_b, h128_in[:, :]), "gq", 16)
            inc(g.dma_start(gw_blk.rearrange("p (a b) -> p a b", a=DTl),
                            gwt.rearrange("(a p) e -> p a e", p=128)), "gq", 16)

        @block.vector
        def _(v):
            inc(v.memset(eps_ap, EPS), "vm")

        def ln_transpose(n_tiles, load_tile, arena_base_for):
            sq0, vm0 = cnt["sq"], cnt["vm"]

            @block.sync
            def _(s):
                for i in range(n_tiles):
                    if i >= 2:
                        s.wait_ge(sems["vm"], vm0 + (i - 1))
                    load_tile(s, i, i % 2)
            sq_per = (cnt["sq"] - sq0) // n_tiles

            @block.vector
            def _(v):
                for i in range(n_tiles):
                    v.wait_ge(sems["sq"], sq0 + sq_per * (i + 1))
                    x = xstage[i % 2]
                    v.reduce_sum(mean_all[:, i:i + 1], x, axis=AX.X)
                    ins = v.scalar_tensor_tensor(xnbf[0], x, 1.0, x,
                                                 op0=ALU.mult, op1=ALU.mult,
                                                 accum_out=qsum_all[:, i:i + 1])
                    inc(ins, "vm")
                v.drain()
                v.tensor_scalar(mean_all[:, 0:n_tiles], mean_all[:, 0:n_tiles],
                                1.0 / D, None, op0=ALU.mult)
                v.tensor_scalar(var_all[:, 0:n_tiles], qsum_all[:, 0:n_tiles],
                                1.0 / D, None, op0=ALU.mult)
                v.drain()
                v.tensor_tensor(gj[:, 0:n_tiles], mean_all[:, 0:n_tiles],
                                mean_all[:, 0:n_tiles], op=ALU.mult)
                v.drain()
                inc(v.tensor_tensor(var_all[:, 0:n_tiles], var_all[:, 0:n_tiles],
                                    gj[:, 0:n_tiles], op=ALU.subtract), "vm")
            vm_stats = cnt["vm"]

            @block.scalar
            def _(sc):
                sc.wait_ge(sems["vm"], vm_stats)
                sc.activation(lnv_all[:, 0:n_tiles], var_all[:, 0:n_tiles], AF.Ln,
                              bias=eps_ap, scale=1.0)
                sc.drain()
                inc(sc.activation(istd_all[:, 0:n_tiles], lnv_all[:, 0:n_tiles],
                                  AF.Exp, scale=-0.5), "sm")
            sm_istd = cnt["sm"]

            @block.vector
            def _(v):
                v.wait_ge(sems["sm"], sm_istd)
                inc(v.scalar_tensor_tensor(nbias_all[:, 0:n_tiles],
                                           mean_all[:, 0:n_tiles], -1.0,
                                           istd_all[:, 0:n_tiles],
                                           op0=ALU.mult, op1=ALU.mult), "vm")
            vm_nb = cnt["vm"]

            sq1 = cnt["sq"]
            xbU0, xbF0, ptU0, ptF0 = cnt["xbU"], cnt["xbF"], cnt["ptU"], cnt["ptF"]

            @block.sync
            def _(s):
                s.wait_ge(sems["vm"], vm_nb)
                for i in range(n_tiles):
                    if i >= 2:
                        s.wait_ge(sems["xbU"], xbU0 + (i - 1))
                    load_tile(s, i, i % 2)

            @block.scalar
            def _(sc):
                for i in range(n_tiles):
                    sc.wait_ge(sems["sq"], sq1 + sq_per * (i + 1))
                    if i >= 2:
                        sc.wait_ge(sems["ptU"], ptU0 + 2 * (i - 1))
                    inc(sc.activation(xnbf[i % 2], xstage[i % 2], AF.Identity,
                                      bias=nbias_all[:, i:i + 1],
                                      scale=istd_all[:, i:i + 1]), "xbU")

            @block.tensor
            def _(t):
                t.wait_ge(sems["gq"], 32)
                for i in range(n_tiles):
                    t.wait_ge(sems["xbU"], xbU0 + (i + 1))
                    for g8 in range(2):
                        gi = 2 * i + g8
                        if gi >= 1:
                            t.wait_ge(sems["ptF"], ptF0 + gi)
                        for j in range(8):
                            dj = g8 * 8 + j
                            ins = t.transpose(pT[:, j * 128:(j + 1) * 128],
                                              xnbf[i % 2][:, dj * 128:(dj + 1) * 128],
                                              ident_b)
                        inc(ins, "ptU")

            @block.vector
            def _(v):
                for i in range(n_tiles):
                    for g8 in range(2):
                        gi = 2 * i + g8
                        v.wait_ge(sems["ptU"], ptU0 + gi + 1)
                        base_b, tcol = arena_base_for(i)
                        dst = sapb(base_b + g8 * 8 * (NT * 2) + tcol * 2,
                                   [[NT, 8], [1, 128]])
                        inc(v.tensor_copy(dst, pT.rearrange("p (a b) -> p a b", a=8)),
                            "ptF")

        if upto >= 1:
            gate()

            def load_x(s, i, slot):
                src = x_own if i < 8 else x_oth
                r_ = (i % 8) * 128
                inc(s.dma_start(xstage[slot], src[r_:r_ + 128, :]), "sq", 16)

            def arena0(i):
                return ((0 if i < 8 else 32 * KB), (i % 8) * 128)

            ln_transpose(16, load_x, arena0)

            dbg_dump([(dbg["dbg_xnT"][:, 0:DTl * NT], cb(bigb, 0, DTl * NT)),
                      (dbg["dbg_xnT"][:, DTl * NT:], cb(bigb, 32 * KB, DTl * NT))]
                     if DEBUG else [])

        if upto >= 2:
            gate()
            gq0, paU0, paF0, kbU0, aq0 = (cnt["gq"], cnt["paU"], cnt["paF"],
                                          cnt["kbU"], cnt["aq"])

            @block.gpsimd
            def _(g):
                for k in range(DTl):
                    inc(g.dma_start(wfull_t(k), wv[k * 128:(k + 1) * 128, :]), "gq", 16)

            @block.tensor
            def _(t):
                t.wait_ge(sems["gq"], gq0 + 16 * DTl)
                for i in range(16):
                    if i >= 1:
                        t.wait_ge(sems["paF"], paF0 + i)
                    half, tc = (0, i * 128) if i < 8 else (1, (i - 8) * 128)
                    for ch in range(4):
                        for k in range(DTl):
                            ins = t.matmul(pA[:, ch * 512:(ch + 1) * 512],
                                           xnT(half, k)[:, tc:tc + 128],
                                           wfull_t(k)[:, ch * 512:(ch + 1) * 512],
                                           start=(k == 0), stop=(k == DTl - 1))
                    inc(ins, "paU")

            @block.vector
            def _(v):
                for i in range(16):
                    v.wait_ge(sems["paU"], paU0 + i + 1)
                    if i >= 1:
                        v.wait_ge(sems["aq"], aq0 + 16 * i)
                    v.tensor_copy(kbf[:, 0:1024], pA[:, 0:1024])
                    inc(v.tensor_copy(kbf[:, 1024:2048], pA[:, 1024:2048]), "kbU")

            @block.scalar
            def _(sc):
                for i in range(16):
                    sc.wait_ge(sems["kbU"], kbU0 + i + 1)
                    inc(sc.dma_start(vspill[i * 128:(i + 1) * 128, :], kbf), "aq", 16)
                    sc.sem_inc(sems["paF"], 1)
                    cnt["paF"] += 1

        def proj_xpos(isq):
            w_in = wq if isq else wk
            cos_in, sin_in = (cosq, sinq) if isq else (cosk, sink)
            ncols = NT if isq else S
            nch = ncols // 512
            gate()
            gq0, sq0, aq0 = cnt["gq"], cnt["sq"], cnt["aq"]
            paU0, pbU0, pbF0, kbU0, vm0, evF0 = (cnt["paU"], cnt["pbU"], cnt["pbF"],
                                                 cnt["kbU"], cnt["vm"], cnt["evF"])

            @block.gpsimd
            def _(g):
                for k in range(DTl):
                    inc(g.dma_start(wfull_t(k), w_in[k * 128:(k + 1) * 128, :]),
                        "gq", 16)

            @block.sync
            def _(s):
                for ht in range(DTl):
                    if ht >= 1:
                        s.wait_ge(sems["vm"], vm0 + ht)
                    inc(s.dma_start(c1stage[:, 0:ncols],
                                    cos_in[ht * 128:(ht + 1) * 128, :]), "sq", 16)
                    inc(s.dma_start(s1stage[:, 0:ncols],
                                    sin_in[ht * 128:(ht + 1) * 128, :]), "sq", 16)

            @block.tensor
            def _(t):
                t.wait_ge(sems["gq"], gq0 + 16 * DTl)
                for ht in range(DTl):
                    if ht >= 1:
                        t.wait_ge(sems["evF"], evF0 + ht)
                    for ch in range(nch):
                        c0 = ch * 512
                        for k in range(DTl):
                            if isq:
                                rhs = xnT(0, k)[:, c0:c0 + 512]
                            else:
                                half = 0 if c0 < NT else 1
                                rhs = xnT(half, k)[:, c0 % NT:c0 % NT + 512]
                            ins = t.matmul(pA[:, c0 % 2048:c0 % 2048 + 512],
                                           wfull_t(k)[:, ht * 128:(ht + 1) * 128],
                                           rhs, start=(k == 0), stop=(k == DTl - 1))
                    inc(ins, "paU")
                    t.wait_ge(sems["evF"], evF0 + ht + 1)
                    for ch in range(nch):
                        gch = ht * nch + ch
                        if gch >= 3:
                            t.wait_ge(sems["pbF"], pbF0 + gch - 2)
                        ins = t.matmul(pB[:, (gch % 3) * 512:(gch % 3) * 512 + 512],
                                       prot_f, kstage[:, ch * 512:ch * 512 + 512],
                                       start=True, stop=True)
                        inc(ins, "pbU")

            @block.vector
            def _(v):
                for ht in range(DTl):
                    v.wait_ge(sems["paU"], paU0 + ht + 1)
                    if isq:
                        inc(v.tensor_copy(kstage[:, 0:1024], pA[:, 0:1024]), "evF")
                    else:
                        v.tensor_copy(kstage[:, 0:1024], pA[:, 0:1024])
                        inc(v.tensor_copy(kstage[:, 1024:2048], pA[:, 1024:2048]),
                            "evF")
                    v.wait_ge(sems["sq"], sq0 + 32 * (ht + 1))
                    if (not isq) and ht >= 1:
                        v.wait_ge(sems["aq"], aq0 + 16 * ht)
                    v.drain()
                    for ch in range(nch):
                        gch = ht * nch + ch
                        c0 = ch * 512
                        v.wait_ge(sems["pbU"], pbU0 + gch + 1)
                        v.tensor_tensor(t1c[ch % 2], kstage[:, c0:c0 + 512],
                                        c1stage[:, c0:c0 + 512], op=ALU.mult)
                        ins = v.tensor_tensor(
                            t2c[ch % 2], pB[:, (gch % 3) * 512:(gch % 3) * 512 + 512],
                            s1stage[:, c0:c0 + 512], op=ALU.mult)
                        inc(ins, "pbF")
                        v.drain()
                        dst = (qt_t(ht)[:, c0:c0 + 512] if isq
                               else kbf[:, c0:c0 + 512])
                        ins = v.tensor_tensor(dst, t1c[ch % 2], t2c[ch % 2],
                                              op=ALU.add)
                        if ch == nch - 1:
                            if isq:
                                inc(ins, "vm")
                            else:
                                inc(ins, "kbU")

            if not isq:
                @block.scalar
                def _(sc):
                    for ht in range(DTl):
                        sc.wait_ge(sems["kbU"], kbU0 + ht + 1)
                        inc(sc.dma_start(ktspill[ht * 128:(ht + 1) * 128, :], kbf),
                            "aq", 16)
                        sc.sem_inc(sems["vm"], 1)
                        cnt["vm"] += 1

        if upto >= 3:
            proj_xpos(False)
        if upto >= 4:
            proj_xpos(True)
            dbg_dump([(dbg["dbg_qt"], cb(bigb, 128 * KB, DTl * NT))] if DEBUG else [])

        if upto >= 5:
            gate()
            sq0, paU0, paF0 = cnt["sq"], cnt["paU"], cnt["paF"]

            @block.sync
            def _(s):
                for mt in range(16):
                    if mt >= 2:
                        s.wait_ge(sems["paF"], paF0 + mt - 1)
                    src = bass.AP(ktspill.tensor, ktspill.offset + mt * 128,
                                  [[S, 128], [128 * S, 16], [1, 128]])
                    inc(s.dma_start(kblk[mt % 2].rearrange("p (a b) -> p a b", a=16),
                                    src), "sq", 16)
                    inc(s.dma_start(dstage[mt % 2],
                                    dt_in[mt * 128:(mt + 1) * 128, :]), "sq", 16)

            @block.tensor
            def _(t):
                for mt in range(16):
                    t.wait_ge(sems["sq"], sq0 + 32 * (mt + 1))
                    if mt >= 2:
                        t.wait_ge(sems["paF"], paF0 + mt - 1)
                    sl = (mt % 2) * 1024
                    for ch2 in range(2):
                        for ht in range(DTl):
                            ins = t.matmul(pA[:, sl + ch2 * 512: sl + ch2 * 512 + 512],
                                           kblk[mt % 2][:, ht * 128:(ht + 1) * 128],
                                           qt_t(ht)[:, ch2 * 512:ch2 * 512 + 512],
                                           start=(ht == 0), stop=(ht == DTl - 1))
                    inc(ins, "paU")

            @block.vector
            def _(v):
                for mt in range(16):
                    v.wait_ge(sems["paU"], paU0 + mt + 1)
                    sl = (mt % 2) * 1024
                    inc(v.tensor_tensor(att_t(mt), pA[:, sl:sl + 1024],
                                        dstage[mt % 2], op=ALU.mult), "paF")

            dbg_dump([(dbg["dbg_att"], cb(bigb, 32 * KB, DTl * NT))] if DEBUG else [])

        if upto >= 6:
            gate()
            sq0, paU0, paF0, tm0 = cnt["sq"], cnt["paU"], cnt["paF"], cnt["tm"]

            @block.sync
            def _(s):
                for g in range(8):
                    if g >= 2:
                        s.wait_ge(sems["paU"], paU0 + g - 1)
                    src = bass.AP(vspill.tensor, vspill.offset + g * 256,
                                  [[D, 128], [128 * D, 16], [1, 256]])
                    inc(s.dma_start(vgblk[g % 2].rearrange("p (a b) -> p a b", a=16),
                                    src), "sq", 16)

            @block.tensor
            def _(t):
                for g in range(8):
                    t.wait_ge(sems["sq"], sq0 + 16 * (g + 1))
                    if g >= 1:
                        t.wait_ge(sems["paF"], paF0 + g)
                    for m in range(16):
                        for dd in range(2):
                            for ch2 in range(2):
                                ins = t.matmul(
                                    pA[:, (dd * 2 + ch2) * 512:(dd * 2 + ch2) * 512 + 512],
                                    vgblk[g % 2][:, m * 256 + dd * 128:m * 256 + dd * 128 + 128],
                                    att_t(m)[:, ch2 * 512:ch2 * 512 + 512],
                                    start=(m == 0), stop=(m == 15))
                    inc(ins, "paU")

            @block.vector
            def _(v):
                for g in range(8):
                    v.wait_ge(sems["paU"], paU0 + g + 1)
                    dst = sapf(96 * KB + (2 * g) * (NT * 4),
                               [[NT, 2], [512, 2], [1, 512]])
                    inc(v.tensor_copy(dst, pA.rearrange("p (a b c) -> p a b c",
                                                        a=2, b=2)), "paF")

            if upto == 6:
                dbg_dump([(dbg["dbg_oacc"], cf(bigf, 96 * KB, DTl * NT))]
                         if DEBUG else [])

        if upto >= 7:
            har_map = list(range(FT))

            def har_phys(pid):
                return har_t(pid) if pid < FT else bfly_sp[pid - FT]

            for ch in range(2):
                gate()
                har_map[:] = list(range(FT))
                c0 = ch * 512
                gq0, paU0, hbU0, evF0, sm0, wsF0, tmA0 = (
                    cnt["gq"], cnt["paU"], cnt["hbU"], cnt["evF"], cnt["sm"],
                    cnt["wsF"], cnt["tm"])

                @block.gpsimd
                def _(g):
                    for j1 in range(FT):
                        if j1 >= 2:
                            g.wait_ge(sems["paU"], paU0 + j1 - 1)
                        for wt, blk in ((w1t, w1blk), (w3t, w3blk)):
                            src = bass.AP(wt.tensor, wt.offset + j1 * 128,
                                          [[FF, 128], [128 * FF, 16], [1, 128]])
                            inc(g.dma_start(
                                blk[j1 % 2].rearrange("p (a b) -> p a b", a=16), src),
                                "gq", 16)

                @block.tensor
                def _(t):
                    for j1 in range(FT):
                        t.wait_ge(sems["gq"], gq0 + 32 * (j1 + 1))
                        if j1 >= 1:
                            t.wait_ge(sems["sm"], sm0 + j1)
                            t.wait_ge(sems["hbU"], hbU0 + j1)
                        for k in range(DTl):
                            t.matmul(pA[:, 0:512],
                                     w1blk[j1 % 2][:, k * 128:(k + 1) * 128],
                                     xnT(0, k)[:, c0:c0 + 512],
                                     start=(k == 0), stop=(k == DTl - 1))
                        for k in range(DTl):
                            ins = t.matmul(pA[:, 512:1024],
                                           w3blk[j1 % 2][:, k * 128:(k + 1) * 128],
                                           xnT(0, k)[:, c0:c0 + 512],
                                           start=(k == 0), stop=(k == DTl - 1))
                        inc(ins, "paU")
                        t.wait_ge(sems["hbU"], hbU0 + j1 + 1)
                        if j1 >= 2:
                            t.wait_ge(sems["evF"], evF0 + j1 - 1)
                        ins = t.matmul(
                            pA[:, 1024 + (j1 % 2) * 512:1536 + (j1 % 2) * 512],
                            h128_b, hbr[j1 % 2], start=True, stop=True)
                        inc(ins, "tm")

                @block.scalar
                def _(sc):
                    for j1 in range(FT):
                        sc.wait_ge(sems["paU"], paU0 + j1 + 1)
                        if j1 >= 2:
                            sc.wait_ge(sems["hbU"], hbU0 + j1 - 1)
                        inc(sc.activation(stmp[j1 % 2], pA[:, 0:512], AF.Silu), "sm")

                @block.vector
                def _(v):
                    for j1 in range(FT):
                        v.wait_ge(sems["sm"], sm0 + j1 + 1)
                        if j1 >= 2:
                            v.wait_ge(sems["tm"], tmA0 + j1 - 1)
                        ins = v.scalar_tensor_tensor(hbr[j1 % 2], stmp[j1 % 2],
                                                     INV_SQRT_FF, pA[:, 512:1024],
                                                     op0=ALU.mult, op1=ALU.mult)
                        inc(ins, "hbU")
                        v.wait_ge(sems["tm"], tmA0 + j1 + 1)
                        ins = v.tensor_copy(
                            har_t(j1),
                            pA[:, 1024 + (j1 % 2) * 512:1536 + (j1 % 2) * 512])
                        inc(ins, "evF")

                @block.vector
                def _(v):
                    v.wait_ge(sems["evF"], cnt["evF"])
                    free = [FT, FT + 1, FT + 2, FT + 3]
                    for st in range(BFLY_STAGES):
                        sig = 1 << st
                        pairs = [(j, j + sig) for b0 in range(0, FT, 2 * sig)
                                 for j in range(b0, b0 + sig)]
                        for (a, b) in pairs:
                            pa_, pb_ = har_map[a], har_map[b]
                            d1, d2 = free[0], free[1]
                            v.tensor_tensor(har_phys(d1), har_phys(pa_),
                                            har_phys(pb_), op=ALU.add)
                            v.tensor_tensor(har_phys(d2), har_phys(pa_),
                                            har_phys(pb_), op=ALU.subtract)
                            free = free[2:] + [pa_, pb_]
                            har_map[a], har_map[b] = d1, d2
                        v.drain()
                    arena_free = [p for p in free if p < FT]
                    for a in range(FT):
                        if har_map[a] >= FT:
                            dstp = arena_free.pop()
                            v.tensor_copy(har_phys(dstp), har_phys(har_map[a]))
                            har_map[a] = dstp
                    inc(v.drain(), "vm")
                vm_bfly = cnt["vm"]
                HAR_MAPS.append(list(har_map))

                gq1, paU1, tmW0, paFW = cnt["gq"], cnt["paU"], cnt["tm"], cnt["paF"]

                def _w2sig(gi):
                    return (gi // 8) * 7 + min(gi % 8 + 1, 7)

                @block.gpsimd
                def _(g):
                    g.wait_ge(sems["vm"], vm_bfly)
                    for ggrp in range(4):
                        for j8 in range(8):
                            gi = ggrp * 8 + j8
                            if gi >= 2:
                                pgi = gi - 2
                                if pgi % 8 == 7:
                                    g.wait_ge(sems["paU"], paU1 + pgi // 8 + 1)
                                else:
                                    g.wait_ge(sems["tm"], tmW0 + _w2sig(pgi))
                            src = bass.AP(w2t.tensor,
                                          w2t.offset + (j8 * 1024) * D + ggrp * 512,
                                          [[D, 128], [128 * D, 8], [1, 512]])
                            inc(g.dma_start(
                                w2blk[gi % 2].rearrange("p (a b) -> p a b", a=8),
                                src), "gq", 16)

                @block.tensor
                def _(t):
                    t.wait_ge(sems["vm"], vm_bfly)
                    for ggrp in range(4):
                        if ggrp >= 1:
                            t.wait_ge(sems["paF"], paFW + ggrp)
                        for j8 in range(8):
                            gi = ggrp * 8 + j8
                            t.wait_ge(sems["gq"], gq1 + 16 * (gi + 1))
                            for js in range(8):
                                j = j8 * 8 + js
                                for dd in range(4):
                                    ins = t.matmul(
                                        pA[:, dd * 512:(dd + 1) * 512],
                                        w2blk[gi % 2][:, js * 512 + dd * 128:js * 512 + dd * 128 + 128],
                                        har_phys(har_map[j]),
                                        start=(j == 0), stop=(j == FT - 1))
                            if j8 < 7:
                                inc(ins, "tm")
                            else:
                                inc(ins, "paU")

                @block.vector
                def _(v):
                    for ggrp in range(4):
                        v.wait_ge(sems["paU"], paU1 + ggrp + 1)
                        dst = sapf(96 * KB + (4 * ggrp) * (NT * 4) + c0 * 4,
                                   [[NT, 4], [1, 512]])
                        ins = v.tensor_tensor(dst, dst,
                                              pA.rearrange("p (a b) -> p a b", a=4),
                                              op=ALU.add)
                        inc(ins, "paF")

            dbg_dump([(dbg["dbg_oacc"], cf(bigf, 96 * KB, DTl * NT)),
                      (dbg["dbg_hid"], cb(bigb, 32 * KB, FT * 512))] if DEBUG else [])

        if upto >= 8:
            gate()
            sq0, paU0, paF0, vm0, aq0 = (cnt["sq"], cnt["paU"], cnt["paF"],
                                         cnt["vm"], cnt["aq"])

            @block.sync
            def _(s):
                for i in range(8):
                    if i >= 2:
                        s.wait_ge(sems["vm"], vm0 + (i - 1))
                    inc(s.dma_start(xstage[i % 2], x_own[i * 128:(i + 1) * 128, :]),
                        "sq", 16)

            @block.tensor
            def _(t):
                for i in range(8):
                    if i >= 1:
                        t.wait_ge(sems["paF"], paF0 + i)
                    for j in range(DTl):
                        ins = t.transpose(pA[:, j * 128:(j + 1) * 128],
                                          oacc_t(j)[:, i * 128:(i + 1) * 128],
                                          ident_f)
                    inc(ins, "paU")

            @block.vector
            def _(v):
                for i in range(8):
                    v.wait_ge(sems["paU"], paU0 + i + 1)
                    v.wait_ge(sems["sq"], sq0 + 16 * (i + 1))
                    if i >= 2:
                        v.wait_ge(sems["aq"], aq0 + 16 * (i - 1))
                    inc(v.tensor_tensor(x1buf[i % 2], xstage[i % 2], pA[:, 0:2048],
                                        op=ALU.add), "vm")

            @block.scalar
            def _(sc):
                for i in range(8):
                    sc.wait_ge(sems["vm"], vm0 + i + 1)
                    inc(sc.dma_start(x1spill[i * 128:(i + 1) * 128, :], x1buf[i % 2]),
                        "aq", 16)
                    sc.sem_inc(sems["paF"], 1)
                    cnt["paF"] += 1

            gate()

            def load_x1(s, i, slot):
                inc(s.dma_start(xstage[slot], x1spill[i * 128:(i + 1) * 128, :]),
                    "sq", 16)

            def arena1(i):
                return (32 * KB, i * 128)

            ln_transpose(8, load_x1, arena1)

            dbg_dump([(dbg["dbg_xn1T"], cb(bigb, 32 * KB, DTl * NT))] if DEBUG else [])

        if upto >= 9:
            gate()

            @block.vector
            def _(v):
                for j in range(4):
                    inc(v.memset(cf(bigf, 96 * KB + j * 16 * KB, 4 * NT), 0.0), "vm")
            zero_vm = cnt["vm"]

            ptU0, ptF0, pbU0, paU0, paF0, sm0, vm0 = (
                cnt["ptU"], cnt["ptF"], cnt["pbU"], cnt["paU"], cnt["paF"],
                cnt["sm"], cnt["vm"])

            @block.tensor
            def _(t):
                it = 0
                for sg in range(2):
                    for hd in range(16):
                        par = hd % 2
                        gT = xn1T(hd)[:, sg * 512 + par: sg * 512 + par + 511:2]
                        if it >= 1:
                            t.wait_ge(sems["ptF"], ptF0 + it)
                        t.transpose(pT[:, 0:128], gT[:, 0:128], ident_b)
                        inc(t.transpose(pT[:, 128:256], gT[:, 128:256], ident_b),
                            "ptU")
                        if it >= 1:
                            t.wait_ge(sems["sm"], sm0 + 2 * it)
                        for q2 in range(2):
                            ins = t.matmul(pB[:, q2 * 256: q2 * 256 + 256],
                                           gT[:, q2 * 128:(q2 + 1) * 128], gT,
                                           start=True, stop=True)
                        inc(ins, "pbU")
                        t.wait_ge(sems["vm"], zero_vm + 3 * it + 2)
                        if it >= 1:
                            t.wait_ge(sems["vm"], zero_vm + 3 * it)
                        for q2 in range(2):
                            for k2 in range(2):
                                ins = t.transpose(
                                    pB_bf[:, 2048 + (k2 * 2 + q2) * 128:
                                          2048 + (k2 * 2 + q2) * 128 + 128],
                                    a_sb[:, q2 * 256 + k2 * 128:
                                         q2 * 256 + k2 * 128 + 128],
                                    ident_b)
                        inc(ins, "ptU")
                        t.wait_ge(sems["vm"], zero_vm + 3 * it + 3)
                        if it >= 1:
                            t.wait_ge(sems["paF"], paF0 + it)
                        for k2 in range(2):
                            ins = t.matmul(pA[:, 0:256],
                                           g_sb[:, k2 * 128:(k2 + 1) * 128],
                                           at_sb[:, k2 * 256:(k2 + 1) * 256],
                                           start=(k2 == 0), stop=(k2 == 1))
                        inc(ins, "paU")
                        it += 1

            @block.scalar
            def _(sc):
                for it in range(32):
                    sc.wait_ge(sems["vm"], zero_vm + 3 * it + 1)
                    for q2 in range(2):
                        inc(sc.activation(e_sb[:, q2 * 256:(q2 + 1) * 256],
                                          pB[:, q2 * 256: q2 * 256 + 256], AF.Exp,
                                          bias=nmax2[:, q2:q2 + 1], scale=L1_SCALE,
                                          accum_out=zden2[:, q2:q2 + 1]), "sm")
                    sc.drain()

            @block.vector
            def _(v):
                v.wait_ge(sems["vm"], zero_vm)
                it = 0
                for sg in range(2):
                    for hd in range(16):
                        par = hd % 2
                        v.wait_ge(sems["pbU"], pbU0 + it + 1)
                        for q2 in range(2):
                            v.reduce_max(nmax2[:, q2:q2 + 1],
                                         pB[:, q2 * 256: q2 * 256 + 256],
                                         axis=AX.X, negate=True)
                        v.drain()
                        inc(v.tensor_scalar(nmax2, nmax2, L1_SCALE, None,
                                            op0=ALU.mult), "vm")
                        v.wait_ge(sems["sm"], sm0 + 2 * it + 2)
                        v.reciprocal(rz2, zden2)
                        v.drain()
                        for q2 in range(2):
                            ins = v.tensor_scalar(a_sb[:, q2 * 256:(q2 + 1) * 256],
                                                  e_sb[:, q2 * 256:(q2 + 1) * 256],
                                                  rz2[:, q2:q2 + 1], None,
                                                  op0=ALU.mult)
                        inc(ins, "vm")
                        v.wait_ge(sems["ptU"], ptU0 + 2 * it + 1)
                        inc(v.tensor_copy(g_sb, pT[:, 0:256]), "ptF")
                        v.wait_ge(sems["ptU"], ptU0 + 2 * it + 2)
                        inc(v.tensor_copy(at_sb, pB_bf[:, 2048:2560]), "vm")
                        v.wait_ge(sems["paU"], paU0 + it + 1)
                        dst = sapf(96 * KB + hd * (NT * 4) + (sg * 512 + par) * 4,
                                   [[2, 256]])
                        inc(v.tensor_tensor(dst, dst, pA[:, 0:256], op=ALU.add),
                            "paF")
                        it += 1

            if upto == 9:
                dbg_dump([(dbg["dbg_out1"], cf(bigf, 96 * KB, DTl * NT))]
                         if DEBUG else [])

        if upto >= 10:
            gate()
            paU0, paF0, vm0, sm0, pbU0 = (cnt["paU"], cnt["paF"], cnt["vm"],
                                          cnt["sm"], cnt["pbU"])

            @block.tensor
            def _(t):
                for i in range(8):
                    if i >= 1:
                        t.wait_ge(sems["paF"], paF0 + i)
                    for k in range(DTl):
                        ins = t.matmul(pA[:, 0:16],
                                       xn1T(k)[:, i * 128:(i + 1) * 128],
                                       gw_blk[:, k * E:(k + 1) * E],
                                       start=(k == 0), stop=(k == DTl - 1))
                    inc(ins, "paU")
                t.wait_ge(sems["vm"], vm0 + 3 * 8)
                for i in range(8):
                    ins = t.transpose(pB[0:16, i * 128:(i + 1) * 128],
                                      cw_all[:, i * E:(i + 1) * E], ident_f)
                inc(ins, "pbU")

            @block.vector
            def _(v):
                for i in range(8):
                    v.wait_ge(sems["paU"], paU0 + i + 1)
                    inc(v.tensor_copy(gj[:, 0:16], pA[:, 0:16]), "paF")
                    v.drain()
                    inc(v.reduce_max(nmax2[:, 0:1], gj[:, 0:16], axis=AX.X,
                                     negate=True), "vm")
                    v.wait_ge(sems["sm"], sm0 + i + 1)
                    v.reciprocal(rz2[:, 0:1], zden2[:, 0:1])
                    v.drain()
                    inc(v.tensor_scalar(cw_all[:, i * E:(i + 1) * E],
                                        e_sb.bitcast(F32)[:, 0:16],
                                        rz2[:, 0:1], None, op0=ALU.mult), "vm")
                    v.drain()
                    v.max(m8t, cw_all[:, i * E:(i + 1) * E])
                    v.drain()
                    v.tensor_scalar(gj[:, 0:16], cw_all[:, i * E:(i + 1) * E],
                                    m8t[:, 5:6], None, op0=ALU.is_ge)
                    v.drain()
                    inc(v.tensor_tensor(cw_all[:, i * E:(i + 1) * E],
                                        cw_all[:, i * E:(i + 1) * E], gj[:, 0:16],
                                        op=ALU.mult), "vm")
                v.wait_ge(sems["pbU"], pbU0 + 1)
                inc(v.tensor_copy(cwT[0:16, :].rearrange("p (a b) -> p a b", a=8),
                                  pB[0:16, 0:1024].rearrange("p (a b) -> p a b",
                                                             a=8)), "pbF")

            @block.scalar
            def _(sc):
                for i in range(8):
                    sc.wait_ge(sems["vm"], vm0 + 3 * i + 1)
                    inc(sc.activation(e_sb.bitcast(F32)[:, 0:16], gj[:, 0:16],
                                      AF.Exp, bias=nmax2[:, 0:1], scale=1.0,
                                      accum_out=zden2[:, 0:1]), "sm")
                    sc.drain()

            dbg_dump([(dbg["dbg_cw"], cw_all)] if DEBUG else [])

        if upto >= 11:
            gate()

            @block.gpsimd
            def _(g):
                inc(g.dma_start(sel_b[0:16, :], sel_in[:, :]), "gq", 16)
            sel_gq = cnt["gq"]

            for pr in range(8):
                gq0, paU0, pbU0, pbF0 = (cnt["gq"], cnt["paU"], cnt["pbU"],
                                         cnt["pbF"])
                sm0, hbU0, wsF0, evF0, tm0 = (cnt["sm"], cnt["hbU"], cnt["wsF"],
                                              cnt["evF"], cnt["tm"])
                paF_pair = cnt["paF"]

                @block.tensor
                def _(t):
                    if pr == 0:
                        t.wait_ge(sems["pbF"], cnt["pbF"])
                        t.wait_ge(sems["gq"], sel_gq)
                    for ee in range(2):
                        e = 2 * pr + ee
                        if ee == 1:
                            t.wait_ge(sems["pbF"], pbF0 + 1)
                        for ch2 in range(2):
                            ins = t.matmul(pB[:, ch2 * 512:(ch2 + 1) * 512],
                                           sel_b[0:16, e * 128:(e + 1) * 128],
                                           cwT[0:16, ch2 * 512:(ch2 + 1) * 512],
                                           start=True, stop=True)
                        inc(ins, "pbU")

                @block.vector
                def _(v):
                    for ee in range(2):
                        v.wait_ge(sems["pbU"], pbU0 + ee + 1)
                        inc(v.tensor_copy(bcw[ee], pB[:, 0:1024]), "pbF")

                @block.gpsimd
                def _(g):
                    for ee in range(2):
                        e = 2 * pr + ee
                        for mo in range(MOT):
                            gi = ee * MOT + mo
                            if gi >= 2:
                                g.wait_ge(sems["paU"], paU0 + gi - 1)
                            for wt, blk in ((wgt, w1blk), (wut, w3blk)):
                                src = bass.AP(wt.tensor,
                                              wt.offset + e * D * MO + mo * 128,
                                              [[MO, 128], [128 * MO, 16], [1, 128]])
                                inc(g.dma_start(
                                    blk[gi % 2].rearrange("p (a b) -> p a b", a=16),
                                    src), "gq", 16)

                @block.tensor
                def _(t):
                    t.wait_ge(sems["paF"], paF_pair)
                    for ee in range(2):
                        for mo in range(MOT):
                            gi = ee * MOT + mo
                            t.wait_ge(sems["gq"], gq0 + 32 * (gi + 1))
                            if gi >= 1:
                                t.wait_ge(sems["sm"], sm0 + gi)
                                t.wait_ge(sems["hbU"], hbU0 + gi)
                            for ch2 in range(2):
                                for k in range(DTl):
                                    t.matmul(pA[:, ch2 * 512:(ch2 + 1) * 512],
                                             w1blk[gi % 2][:, k * 128:(k + 1) * 128],
                                             xn1T(k)[:, ch2 * 512:(ch2 + 1) * 512],
                                             start=(k == 0), stop=(k == DTl - 1))
                            for ch2 in range(2):
                                for k in range(DTl):
                                    ins = t.matmul(
                                        pA[:, 1024 + ch2 * 512:1536 + ch2 * 512],
                                        w3blk[gi % 2][:, k * 128:(k + 1) * 128],
                                        xn1T(k)[:, ch2 * 512:(ch2 + 1) * 512],
                                        start=(k == 0), stop=(k == DTl - 1))
                            inc(ins, "paU")

                @block.scalar
                def _(sc):
                    for gi in range(2 * MOT):
                        sc.wait_ge(sems["paU"], paU0 + gi + 1)
                        if gi >= 2:
                            sc.wait_ge(sems["hbU"], hbU0 + gi - 1)
                        inc(sc.activation(stmp2[gi % 2], pA[:, 0:1024], AF.Silu),
                            "sm")

                @block.vector
                def _(v):
                    for ee in range(2):
                        for mo in range(MOT):
                            gi = ee * MOT + mo
                            v.wait_ge(sems["sm"], sm0 + gi + 1)
                            v.tensor_tensor(hidT_t(gi), stmp2[gi % 2],
                                            pA[:, 1024:2048], op=ALU.mult)
                            v.drain()
                            inc(v.tensor_tensor(hidT_t(gi), hidT_t(gi), bcw[ee],
                                                op=ALU.mult), "hbU")

                gq1, paU1 = cnt["gq"], cnt["paU"]
                paF_dn = cnt["paF"]

                def _dnsig(gi):
                    return (gi // 4) * 3 + min(gi % 4 + 1, 3)

                @block.gpsimd
                def _(g):
                    for dpair in range(8):
                        for ee in range(2):
                            for dd in range(2):
                                e = 2 * pr + ee
                                dti = dpair * 2 + dd
                                gi = dpair * 4 + ee * 2 + dd
                                if gi >= 2:
                                    pgi = gi - 2
                                    if pgi % 4 == 3:
                                        g.wait_ge(sems["paU"],
                                                  paU1 + pgi // 4 + 1)
                                    else:
                                        g.wait_ge(sems["evF"],
                                                  evF0 + _dnsig(pgi))
                                src = bass.AP(wdt.tensor,
                                              wdt.offset + e * MO * D + dti * 128,
                                              [[D, 128], [128 * D, MOT], [1, 128]])
                                inc(g.dma_start(
                                    wdblk[gi % 2].rearrange("p (a b) -> p a b",
                                                            a=MOT), src), "gq", 16)

                @block.tensor
                def _(t):
                    t.wait_ge(sems["hbU"], hbU0 + 2 * MOT)
                    for dpair in range(8):
                        if dpair >= 1:
                            t.wait_ge(sems["paF"], paF_dn + dpair)
                        for ee in range(2):
                            for dd in range(2):
                                gi = dpair * 4 + ee * 2 + dd
                                t.wait_ge(sems["gq"], gq1 + 16 * (gi + 1))
                                for mo in range(MOT):
                                    for ch2 in range(2):
                                        ins = t.matmul(
                                            pA[:, dd * 1024 + ch2 * 512:
                                               dd * 1024 + ch2 * 512 + 512],
                                            wdblk[gi % 2][:, mo * 128:(mo + 1) * 128],
                                            hidT_t(ee * MOT + mo)[:, ch2 * 512:(ch2 + 1) * 512],
                                            start=(ee == 0 and mo == 0),
                                            stop=(ee == 1 and mo == MOT - 1))
                                if gi % 4 == 3:
                                    inc(ins, "paU")
                                else:
                                    inc(ins, "evF")

                @block.vector
                def _(v):
                    for dpair in range(8):
                        v.wait_ge(sems["paU"], paU1 + dpair + 1)
                        dst = sapf(96 * KB + (2 * dpair) * (NT * 4),
                                   [[NT, 2], [1, 1024]])
                        inc(v.tensor_tensor(dst, dst,
                                            pA.rearrange("p (a b) -> p a b", a=2),
                                            op=ALU.add), "paF")

            dbg_dump([(dbg["dbg_out1"], cf(bigf, 96 * KB, DTl * NT))] if DEBUG else [])

        if upto >= 12:
            gate()
            sq0, paU0, paF0, vm0, aq0 = (cnt["sq"], cnt["paU"], cnt["paF"],
                                         cnt["vm"], cnt["aq"])

            @block.sync
            def _(s):
                for i in range(8):
                    if i >= 2:
                        s.wait_ge(sems["vm"], vm0 + (i - 1))
                    inc(s.dma_start(xstage[i % 2], x1spill[i * 128:(i + 1) * 128, :]),
                        "sq", 16)

            @block.tensor
            def _(t):
                for i in range(8):
                    if i >= 1:
                        t.wait_ge(sems["paF"], paF0 + i)
                    for j in range(DTl):
                        ins = t.transpose(pA[:, j * 128:(j + 1) * 128],
                                          out1_t(j)[:, i * 128:(i + 1) * 128],
                                          ident_f)
                    inc(ins, "paU")

            @block.vector
            def _(v):
                for i in range(8):
                    v.wait_ge(sems["paU"], paU0 + i + 1)
                    v.wait_ge(sems["sq"], sq0 + 16 * (i + 1))
                    if i >= 2:
                        v.wait_ge(sems["aq"], aq0 + 16 * (i - 1))
                    inc(v.tensor_tensor(ybuf[i % 2], xstage[i % 2], pA[:, 0:2048],
                                        op=ALU.add), "vm")

            @block.scalar
            def _(sc):
                for i in range(8):
                    sc.wait_ge(sems["vm"], vm0 + i + 1)
                    inc(sc.dma_start(y_out[i * 128:(i + 1) * 128, :], ybuf[i % 2]),
                        "aq", 16)
                    sc.sem_inc(sems["paF"], 1)
                    cnt["paF"] += 1
                sc.wait_ge(sems["aq"], cnt["aq"])

        @block.sync
        def _(s):
            s.wait_ge(sems["aq"], cnt["aq"])

    return nc


def make_in_maps(L0_attn_X, L0_attn_W_Q, L0_attn_W_K, L0_attn_W_V,
                 L0_ffn_W1, L0_ffn_W2, L0_ffn_W3,
                 L1_ffn_gate_w, L1_ffn_w_gate, L1_ffn_w_up, L1_ffn_w_down):
    X = np.ascontiguousarray(np.asarray(L0_attn_X, dtype=np.float32))
    wq = np.ascontiguousarray(np.asarray(L0_attn_W_Q, dtype=np.float32))
    wk = np.ascontiguousarray(np.asarray(L0_attn_W_K, dtype=np.float32))
    wv = np.ascontiguousarray(np.asarray(L0_attn_W_V, dtype=np.float32))
    w1t = np.ascontiguousarray(np.asarray(L0_ffn_W1, dtype=np.float32).T)
    w3t = np.ascontiguousarray(np.asarray(L0_ffn_W3, dtype=np.float32).T)
    w2t = np.ascontiguousarray(np.asarray(L0_ffn_W2, dtype=np.float32).T)
    gwt = np.ascontiguousarray(np.asarray(L1_ffn_gate_w, dtype=np.float32).T)
    wgt = np.ascontiguousarray(
        np.asarray(L1_ffn_w_gate, dtype=np.float32).transpose(0, 2, 1))
    wut = np.ascontiguousarray(
        np.asarray(L1_ffn_w_up, dtype=np.float32).transpose(0, 2, 1))
    wdt = np.ascontiguousarray(
        np.asarray(L1_ffn_w_down, dtype=np.float32).transpose(0, 2, 1))

    cosQ, sinQ = _xpos_tables(False)
    cosK, sinK = _xpos_tables(True)
    DmT = _decay_T()
    h128 = _hadamard(128)
    prot = _prot_T()
    ident = np.eye(128, dtype=np.float32)
    ones1 = np.ones((1, 128), dtype=np.float32)
    sel = np.zeros((16, D), dtype=np.float32)
    for e_ in range(16):
        sel[e_, e_ * 128:(e_ + 1) * 128] = 1.0

    in_maps = []
    for c in range(8):
        b, h = c // 2, c % 2
        own = slice(h * NT, (h + 1) * NT)
        perm = np.r_[h * NT:(h + 1) * NT, (1 - h) * NT:(2 - h) * NT]
        im = {
            "x_own": np.ascontiguousarray(X[b, own]),
            "x_oth": np.ascontiguousarray(X[b, slice((1 - h) * NT, (2 - h) * NT)]),
            "wq": wq, "wk": wk, "wv": wv,
            "w1t": w1t, "w3t": w3t, "w2t": w2t,
            "gwt": gwt, "wgt": wgt, "wut": wut, "wdt": wdt,
            "cosq": np.ascontiguousarray(cosQ[own].T),
            "sinq": np.ascontiguousarray(sinQ[own].T),
            "cosk": np.ascontiguousarray(cosK[perm].T),
            "sink": np.ascontiguousarray(sinK[perm].T),
            "dt": np.ascontiguousarray(DmT[perm][:, own]),
            "ident": ident, "h128": h128, "prot": prot, "ones1": ones1, "sel": sel,
        }
        in_maps.append(im)
    return in_maps


def kernel(L0_attn_X, L0_attn_W_Q, L0_attn_W_K, L0_attn_W_V,
           L0_ffn_W1, L0_ffn_W2, L0_ffn_W3,
           L1_ffn_gate_w, L1_ffn_w_gate, L1_ffn_w_up, L1_ffn_w_down):
    in_maps = make_in_maps(L0_attn_X, L0_attn_W_Q, L0_attn_W_K, L0_attn_W_V,
                           L0_ffn_W1, L0_ffn_W2, L0_ffn_W3,
                           L1_ffn_gate_w, L1_ffn_w_gate, L1_ffn_w_up, L1_ffn_w_down)
    nc = build_program()
    res = run_bass_kernel_spmd(nc, in_maps, core_ids=list(range(8)))
    out = np.empty((B, S, D), dtype=np.float32)
    for c in range(8):
        b, h = c // 2, c % 2
        out[b, h * NT:(h + 1) * NT] = res.results[c]["y"]
    return out


# version 03
# speedup vs baseline: 1.1156x; 1.1156x over previous
import math
import numpy as np
import ml_dtypes
import concourse.bass as bass
import concourse.mybir as mybir
from concourse.bass_utils import run_bass_kernel_spmd

F32 = mybir.dt.float32
BF16 = mybir.dt.bfloat16
U8 = mybir.dt.uint8
AF = mybir.ActivationFunctionType
ALU = mybir.AluOpType
AX = mybir.AxisListType

D = 2048
S = 2048
B = 4
NT = 1024
DTl = 16
FF = 8192
FT = 64
MO = 1408
MOT = 11
E = 16
GAMMA = 0.96875
EPS = 1e-5
INV_SQRT_FF = 1.0 / math.sqrt(FF)
L1_SCALE = 1.0 / math.sqrt(128.0)
KB = 1024

DEBUG = False
UPTO = 99
HAR_MAPS = []
BFLY_STAGES = 6


def _xpos_tables(downscale):
    d = D
    half = d // 2
    sv = (np.arange(0, d, 2, dtype=np.float64) + 0.4 * d) / (1.4 * d)
    power = np.arange(S, dtype=np.float64) / 512.0
    scale = sv[None, :] ** power[:, None]
    if downscale:
        scale = 1.0 / scale
    inv_freq = 1.0 / 10000.0 ** (np.arange(half, dtype=np.float64) / half)
    sinusoid = np.arange(S, dtype=np.float64)[:, None] * inv_freq[None, :]
    sin_s = np.repeat(np.sin(sinusoid) * scale, 2, axis=-1)
    cos_s = np.repeat(np.cos(sinusoid) * scale, 2, axis=-1)
    return cos_s.astype(np.float32), sin_s.astype(np.float32)


def _decay_T():
    n = np.arange(S, dtype=np.int64)
    diff = n[None, :] - n[:, None]
    Dm = np.where(diff >= 0, GAMMA ** np.maximum(diff, 0).astype(np.float64), 0.0)
    return Dm.astype(np.float32)


def _hadamard(n):
    h = np.array([[1.0]])
    while h.shape[0] < n:
        h = np.block([[h, h], [h, -h]])
    return h.astype(np.float32)


def _prot_T():
    M = np.zeros((128, 128), dtype=np.float32)
    j = np.arange(64)
    M[2 * j, 2 * j + 1] = -1.0
    M[2 * j + 1, 2 * j] = 1.0
    return np.ascontiguousarray(M.T)


def build_program(upto=None):
    HAR_MAPS.clear()
    if upto is None:
        upto = UPTO
    nc = bass.Bass("TRN2", target_bir_lowering=False)

    def din(name, shape):
        return nc.dram_tensor(name, shape, F32, kind="ExternalInput").ap()

    x_own = din("x_own", [NT, D])
    x_oth = din("x_oth", [NT, D])
    wq = din("wq", [D, D])
    wk = din("wk", [D, D])
    wv = din("wv", [D, D])
    w1t = din("w1t", [D, FF])
    w3t = din("w3t", [D, FF])
    w2t = din("w2t", [FF, D])
    gwt = din("gwt", [D, E])
    wgt = din("wgt", [E, D, MO])
    wut = din("wut", [E, D, MO])
    wdt = din("wdt", [E, MO, D])
    cosq = din("cosq", [D, NT])
    sinq = din("sinq", [D, NT])
    cosk = din("cosk", [D, S])
    sink = din("sink", [D, S])
    dt_in = din("dt", [S, NT])
    ident_in = din("ident", [128, 128])
    h128_in = din("h128", [128, 128])
    prot_in = din("prot", [128, 128])
    ones_in = din("ones1", [1, 128])
    sel_in = din("sel", [16, D])

    y_out = nc.dram_tensor("y", [NT, D], F32, kind="ExternalOutput").ap()

    sk = "ExternalOutput" if DEBUG else "Internal"
    vspill = nc.dram_tensor("vspill", [S, D], BF16, kind=sk).ap()
    ktspill = nc.dram_tensor("ktspill", [D, S], BF16, kind=sk).ap()
    x1spill = nc.dram_tensor("x1spill", [NT, D], F32, kind=sk).ap()

    dbg = {}
    if DEBUG:
        for name, w, dt in [("dbg_xnT", 2 * DTl * NT, BF16), ("dbg_qt", DTl * NT, BF16),
                            ("dbg_att", DTl * NT, BF16), ("dbg_oacc", DTl * NT, F32),
                            ("dbg_xn1T", DTl * NT, BF16), ("dbg_out1", DTl * NT, F32),
                            ("dbg_cw", 8 * E, F32), ("dbg_hid", FT * 512, BF16)]:
            dbg[name] = nc.dram_tensor(name, [128, w], dt, kind="ExternalOutput").ap()

    sems, cnt = {}, {}
    for s in ["sq", "gq", "aq", "tm", "vm", "sm", "paU", "paF", "pbU", "pbF",
              "ptU", "ptF", "xbU", "xbF", "hbU", "wsF", "evF", "kbU"]:
        sems[s] = nc.semaphore(s).__enter__()
        cnt[s] = 0

    def inc(instr, s, k=1):
        cnt[s] += k
        return instr.then_inc(sems[s], k)

    r0 = nc.sbuf_tensor("r0", [128, 6 * KB], U8).__enter__()
    rs = nc.sbuf_tensor("rs", [128, 40 * KB], U8).__enter__()
    big = nc.sbuf_tensor("big", [128, 160 * KB], U8).__enter__()
    r0f, r0b = r0.bitcast(F32), r0.bitcast(BF16)
    rsf, rsb = rs.bitcast(F32), rs.bitcast(BF16)
    bigf, bigb = big.bitcast(F32), big.bitcast(BF16)
    BIG_FW = 160 * KB // 4
    BIG_BW = 160 * KB // 2
    RS_FW = 40 * KB // 4
    RS_BW = 40 * KB // 2

    def cf(view, off_b, n):
        return view[:, off_b // 4: off_b // 4 + n]

    def cb(view, off_b, n):
        return view[:, off_b // 2: off_b // 2 + n]

    def sapf(off_b, dims):
        return bass.AP(bigf, off_b // 4, [[BIG_FW, 128]] + dims)

    def sapb(off_b, dims):
        return bass.AP(bigb, off_b // 2, [[BIG_BW, 128]] + dims)

    ident_f = cf(r0f, 0, 128)
    prot_f = cf(r0f, 512, 128)
    ones_f = cf(r0f, 1024, 128)[0:1, :]
    eps_ap = cf(r0f, 1536, 1)
    ident_b = cb(r0b, 2560, 128)
    h128_b = cb(r0b, 2816, 128)
    mean_all = cf(r0f, 3072, 16)
    qsum_all = cf(r0f, 3136, 16)
    var_all = cf(r0f, 3200, 16)
    lnv_all = cf(r0f, 3264, 16)
    istd_all = cf(r0f, 3328, 16)
    nbias_all = cf(r0f, 3392, 16)
    nmax2 = cf(r0f, 3456, 2)
    zden2 = cf(r0f, 3520, 2)
    rz2 = cf(r0f, 3584, 2)
    m8t = cf(r0f, 3648, 8)
    gj = cf(r0f, 3712, 16)
    gw_blk = cb(r0b, 4096, DTl * E)

    xstage = [cf(rsf, 0, D), cf(rsf, 8 * KB, D)]
    xnbf = [cb(rsb, 16 * KB, D), cb(rsb, 20 * KB, D)]
    kstage = cf(rsf, 0, D)
    c1stage = cf(rsf, 8 * KB, D)
    s1stage = cf(rsf, 16 * KB, D)
    kbf = cb(rsb, 24 * KB, D)
    t1c = [cf(rsf, 28 * KB, 512), cf(rsf, 30 * KB, 512)]
    t2c = [cf(rsf, 32 * KB, 512), cf(rsf, 34 * KB, 512)]
    kblk = [cb(rsb, 0, D), cb(rsb, 4 * KB, D)]
    dstage = [cf(rsf, 8 * KB, NT), cf(rsf, 12 * KB, NT)]
    vgblk = [cb(rsb, 0, 4096), cb(rsb, 8 * KB, 4096)]
    w1blk = [cb(rsb, 0, D), cb(rsb, 4 * KB, D)]
    w3blk = [cb(rsb, 8 * KB, D), cb(rsb, 12 * KB, D)]
    w2blk = [cb(rsb, 16 * KB, 4096), cb(rsb, 24 * KB, 4096)]
    stmp = [cf(rsf, 32 * KB, 512), cf(rsf, 34 * KB, 512)]
    hbr = [cb(rsb, 36 * KB, 512), cb(rsb, 37 * KB, 512),
           cb(rsb, 38 * KB, 512)]
    bfly_sp = [cb(rsb, 24 * KB + q * KB, 512) for q in range(4)]
    x1buf = [cf(rsf, 24 * KB, D), cf(rsf, 32 * KB, D)]
    e_sb = cb(rsb, 0, 512)
    a_sb = cb(rsb, 1 * KB, 512)
    at_sb = cb(rsb, 2 * KB, 512)
    g_sb = cb(rsb, 3 * KB, 256)
    cw_all = cf(rsf, 4 * KB, 8 * E)
    cwT = cb(rsb, 20 * KB, NT)
    bcw = [cb(rsb, 16 * KB, NT), cb(rsb, 18 * KB, NT)]
    wdblk = [cb(rsb, 22 * KB, MOT * 128), cb(rsb, 25 * KB, MOT * 128)]
    stmp2 = [cf(rsf, 28 * KB, NT), cf(rsf, 32 * KB, NT)]
    sel_b = cb(rsb, 36 * KB, D)
    ybuf = [cf(rsf, 16 * KB, D), cf(rsf, 24 * KB, D)]

    def xnT(half, k):
        return cb(bigb, (0 if half == 0 else 32 * KB) + k * (NT * 2), NT)

    def wfull_t(k):
        return cb(bigb, 64 * KB + k * (D * 2), D)

    def qt_t(k):
        return cb(bigb, 128 * KB + k * (NT * 2), NT)

    def att_t(m):
        return cb(bigb, 32 * KB + m * (NT * 2), NT)

    def oacc_t(k):
        return cf(bigf, 96 * KB + k * (NT * 4), NT)

    HAR0 = 32 * KB

    def har_t(j):
        return cb(bigb, HAR0 + j * KB, 512)

    def xn1T(k):
        return cb(bigb, 32 * KB + k * (NT * 2), NT)

    def out1_t(k):
        return cf(bigf, 96 * KB + k * (NT * 4), NT)

    def hidT_t(j):
        if j < 16:
            return cb(bigb, j * (NT * 2), NT)
        return cb(bigb, 64 * KB + (j - 16) * (NT * 2), NT)

    pA = nc.psum_tensor("pA", [128, 2048], F32).__enter__()
    pB = nc.psum_tensor("pB", [128, 1536], F32).__enter__()
    pT = nc.psum_tensor("pT", [128, 1024], BF16).__enter__()
    pB_bf = pB.bitcast(BF16)

    with nc.Block() as block:

        def gate(skip=()):
            vals = {s: cnt[s] for s in sems if cnt[s] > 0}
            for en in ("sync", "tensor", "vector", "scalar", "gpsimd"):
                if en in skip:
                    continue
                @getattr(block, en)
                def _(e, vals=vals):
                    for s, v in vals.items():
                        e.wait_ge(sems[s], v)

        def dbg_dump(pairs):
            if not DEBUG:
                return
            gate()
            @block.scalar
            def _(sc):
                for dst, src in pairs:
                    inc(sc.dma_start(dst, src), "aq", 16)
                sc.wait_ge(sems["aq"], cnt["aq"])
            gate()

        @block.sync
        def _(s):
            inc(s.dma_start(ident_f, ident_in[:, :]), "sq", 16)
            inc(s.dma_start(prot_f, prot_in[:, :]), "sq", 16)
            inc(s.dma_start(ones_f, ones_in[:, :]), "sq", 16)

        @block.gpsimd
        def _(g):
            inc(g.dma_start(ident_b, ident_in[:, :]), "gq", 16)
            inc(g.dma_start(h128_b, h128_in[:, :]), "gq", 16)
            inc(g.dma_start(gw_blk.rearrange("p (a b) -> p a b", a=DTl),
                            gwt.rearrange("(a p) e -> p a e", p=128)), "gq", 16)

        @block.vector
        def _(v):
            inc(v.memset(eps_ap, EPS), "vm")

        def ln_transpose(n_tiles, load_tile, arena_base_for):
            sq0, vm0 = cnt["sq"], cnt["vm"]

            @block.sync
            def _(s):
                for i in range(n_tiles):
                    if i >= 2:
                        s.wait_ge(sems["vm"], vm0 + (i - 1))
                    load_tile(s, i, i % 2)
            sq_per = (cnt["sq"] - sq0) // n_tiles

            @block.vector
            def _(v):
                for i in range(n_tiles):
                    v.wait_ge(sems["sq"], sq0 + sq_per * (i + 1))
                    x = xstage[i % 2]
                    v.reduce_sum(mean_all[:, i:i + 1], x, axis=AX.X)
                    ins = v.scalar_tensor_tensor(xnbf[0], x, 1.0, x,
                                                 op0=ALU.mult, op1=ALU.mult,
                                                 accum_out=qsum_all[:, i:i + 1])
                    inc(ins, "vm")
                v.drain()
                v.tensor_scalar(mean_all[:, 0:n_tiles], mean_all[:, 0:n_tiles],
                                1.0 / D, None, op0=ALU.mult)
                v.tensor_scalar(var_all[:, 0:n_tiles], qsum_all[:, 0:n_tiles],
                                1.0 / D, None, op0=ALU.mult)
                v.drain()
                v.tensor_tensor(gj[:, 0:n_tiles], mean_all[:, 0:n_tiles],
                                mean_all[:, 0:n_tiles], op=ALU.mult)
                v.drain()
                inc(v.tensor_tensor(var_all[:, 0:n_tiles], var_all[:, 0:n_tiles],
                                    gj[:, 0:n_tiles], op=ALU.subtract), "vm")
            vm_stats = cnt["vm"]

            @block.scalar
            def _(sc):
                sc.wait_ge(sems["vm"], vm_stats)
                sc.activation(lnv_all[:, 0:n_tiles], var_all[:, 0:n_tiles], AF.Ln,
                              bias=eps_ap, scale=1.0)
                sc.drain()
                inc(sc.activation(istd_all[:, 0:n_tiles], lnv_all[:, 0:n_tiles],
                                  AF.Exp, scale=-0.5), "sm")
            sm_istd = cnt["sm"]

            @block.vector
            def _(v):
                v.wait_ge(sems["sm"], sm_istd)
                inc(v.scalar_tensor_tensor(nbias_all[:, 0:n_tiles],
                                           mean_all[:, 0:n_tiles], -1.0,
                                           istd_all[:, 0:n_tiles],
                                           op0=ALU.mult, op1=ALU.mult), "vm")
            vm_nb = cnt["vm"]

            sq1 = cnt["sq"]
            xbU0, xbF0, ptU0, ptF0 = cnt["xbU"], cnt["xbF"], cnt["ptU"], cnt["ptF"]

            @block.sync
            def _(s):
                s.wait_ge(sems["vm"], vm_nb)
                for i in range(n_tiles):
                    if i >= 2:
                        s.wait_ge(sems["xbU"], xbU0 + (i - 1))
                    load_tile(s, i, i % 2)

            @block.scalar
            def _(sc):
                for i in range(n_tiles):
                    sc.wait_ge(sems["sq"], sq1 + sq_per * (i + 1))
                    if i >= 2:
                        sc.wait_ge(sems["ptU"], ptU0 + 2 * (i - 1))
                    inc(sc.activation(xnbf[i % 2], xstage[i % 2], AF.Identity,
                                      bias=nbias_all[:, i:i + 1],
                                      scale=istd_all[:, i:i + 1]), "xbU")

            @block.tensor
            def _(t):
                t.wait_ge(sems["gq"], 32)
                for i in range(n_tiles):
                    t.wait_ge(sems["xbU"], xbU0 + (i + 1))
                    for g8 in range(2):
                        gi = 2 * i + g8
                        if gi >= 1:
                            t.wait_ge(sems["ptF"], ptF0 + gi)
                        for j in range(8):
                            dj = g8 * 8 + j
                            ins = t.transpose(pT[:, j * 128:(j + 1) * 128],
                                              xnbf[i % 2][:, dj * 128:(dj + 1) * 128],
                                              ident_b)
                        inc(ins, "ptU")

            @block.vector
            def _(v):
                for i in range(n_tiles):
                    for g8 in range(2):
                        gi = 2 * i + g8
                        v.wait_ge(sems["ptU"], ptU0 + gi + 1)
                        base_b, tcol = arena_base_for(i)
                        dst = sapb(base_b + g8 * 8 * (NT * 2) + tcol * 2,
                                   [[NT, 8], [1, 128]])
                        inc(v.tensor_copy(dst, pT.rearrange("p (a b) -> p a b", a=8)),
                            "ptF")

        if upto >= 1:
            gate()

            def load_x(s, i, slot):
                src = x_own if i < 8 else x_oth
                r_ = (i % 8) * 128
                inc(s.dma_start(xstage[slot], src[r_:r_ + 128, :]), "sq", 16)

            def arena0(i):
                return ((0 if i < 8 else 32 * KB), (i % 8) * 128)

            ln_transpose(16, load_x, arena0)

            dbg_dump([(dbg["dbg_xnT"][:, 0:DTl * NT], cb(bigb, 0, DTl * NT)),
                      (dbg["dbg_xnT"][:, DTl * NT:], cb(bigb, 32 * KB, DTl * NT))]
                     if DEBUG else [])

        if upto >= 2:
            gq0, paU0, paF0, kbU0, aq0 = (cnt["gq"], cnt["paU"], cnt["paF"],
                                          cnt["kbU"], cnt["aq"])
            p1_ptF_end = cnt["ptF"]

            @block.gpsimd
            def _(g):
                for k in range(DTl):
                    inc(g.dma_start(wfull_t(k), wv[k * 128:(k + 1) * 128, :]), "gq", 16)

            @block.tensor
            def _(t):
                t.wait_ge(sems["gq"], gq0 + 16 * DTl)
                for i in range(16):
                    t.wait_ge(sems["ptF"], p1_ptF_end - 32 + 2 * (i + 1))
                    if i >= 1:
                        t.wait_ge(sems["paF"], paF0 + i)
                    half, tc = (0, i * 128) if i < 8 else (1, (i - 8) * 128)
                    for ch in range(4):
                        for k in range(DTl):
                            ins = t.matmul(pA[:, ch * 512:(ch + 1) * 512],
                                           xnT(half, k)[:, tc:tc + 128],
                                           wfull_t(k)[:, ch * 512:(ch + 1) * 512],
                                           start=(k == 0), stop=(k == DTl - 1))
                    inc(ins, "paU")

            @block.vector
            def _(v):
                for i in range(16):
                    v.wait_ge(sems["paU"], paU0 + i + 1)
                    if i >= 1:
                        v.wait_ge(sems["aq"], aq0 + 16 * i)
                    v.tensor_copy(kbf[:, 0:1024], pA[:, 0:1024])
                    inc(v.tensor_copy(kbf[:, 1024:2048], pA[:, 1024:2048]), "kbU")

            @block.scalar
            def _(sc):
                for i in range(16):
                    sc.wait_ge(sems["kbU"], kbU0 + i + 1)
                    inc(sc.dma_start(vspill[i * 128:(i + 1) * 128, :], kbf), "aq", 16)
                    sc.sem_inc(sems["paF"], 1)
                    cnt["paF"] += 1

        def proj_xpos(isq):
            w_in = wq if isq else wk
            cos_in, sin_in = (cosq, sinq) if isq else (cosk, sink)
            ncols = NT if isq else S
            nch = ncols // 512
            wfull_free_at = cnt["paU"]
            gate(skip=("gpsimd",))
            gq0, sq0, aq0 = cnt["gq"], cnt["sq"], cnt["aq"]
            paU0, pbU0, pbF0, kbU0, vm0, evF0 = (cnt["paU"], cnt["pbU"], cnt["pbF"],
                                                 cnt["kbU"], cnt["vm"], cnt["evF"])

            @block.gpsimd
            def _(g):
                g.wait_ge(sems["paU"], wfull_free_at)
                for k in range(DTl):
                    inc(g.dma_start(wfull_t(k), w_in[k * 128:(k + 1) * 128, :]),
                        "gq", 16)

            @block.sync
            def _(s):
                for ht in range(DTl):
                    if ht >= 1:
                        s.wait_ge(sems["vm"], vm0 + ht)
                    inc(s.dma_start(c1stage[:, 0:ncols],
                                    cos_in[ht * 128:(ht + 1) * 128, :]), "sq", 16)
                    inc(s.dma_start(s1stage[:, 0:ncols],
                                    sin_in[ht * 128:(ht + 1) * 128, :]), "sq", 16)

            @block.tensor
            def _(t):
                t.wait_ge(sems["gq"], gq0 + 16 * DTl)
                for ht in range(DTl):
                    if ht >= 1:
                        t.wait_ge(sems["evF"], evF0 + ht)
                    for ch in range(nch):
                        c0 = ch * 512
                        for k in range(DTl):
                            if isq:
                                rhs = xnT(0, k)[:, c0:c0 + 512]
                            else:
                                half = 0 if c0 < NT else 1
                                rhs = xnT(half, k)[:, c0 % NT:c0 % NT + 512]
                            ins = t.matmul(pA[:, c0 % 2048:c0 % 2048 + 512],
                                           wfull_t(k)[:, ht * 128:(ht + 1) * 128],
                                           rhs, start=(k == 0), stop=(k == DTl - 1))
                    inc(ins, "paU")
                    t.wait_ge(sems["evF"], evF0 + ht + 1)
                    for ch in range(nch):
                        gch = ht * nch + ch
                        if gch >= 3:
                            t.wait_ge(sems["pbF"], pbF0 + gch - 2)
                        ins = t.matmul(pB[:, (gch % 3) * 512:(gch % 3) * 512 + 512],
                                       prot_f, kstage[:, ch * 512:ch * 512 + 512],
                                       start=True, stop=True)
                        inc(ins, "pbU")

            @block.vector
            def _(v):
                for ht in range(DTl):
                    v.wait_ge(sems["paU"], paU0 + ht + 1)
                    if isq:
                        inc(v.tensor_copy(kstage[:, 0:1024], pA[:, 0:1024]), "evF")
                    else:
                        v.tensor_copy(kstage[:, 0:1024], pA[:, 0:1024])
                        inc(v.tensor_copy(kstage[:, 1024:2048], pA[:, 1024:2048]),
                            "evF")
                    v.wait_ge(sems["sq"], sq0 + 32 * (ht + 1))
                    if (not isq) and ht >= 1:
                        v.wait_ge(sems["aq"], aq0 + 16 * ht)
                    v.drain()
                    for ch in range(nch):
                        gch = ht * nch + ch
                        c0 = ch * 512
                        v.wait_ge(sems["pbU"], pbU0 + gch + 1)
                        v.tensor_tensor(t1c[ch % 2], kstage[:, c0:c0 + 512],
                                        c1stage[:, c0:c0 + 512], op=ALU.mult)
                        ins = v.tensor_tensor(
                            t2c[ch % 2], pB[:, (gch % 3) * 512:(gch % 3) * 512 + 512],
                            s1stage[:, c0:c0 + 512], op=ALU.mult)
                        inc(ins, "pbF")
                        v.drain()
                        dst = (qt_t(ht)[:, c0:c0 + 512] if isq
                               else kbf[:, c0:c0 + 512])
                        ins = v.tensor_tensor(dst, t1c[ch % 2], t2c[ch % 2],
                                              op=ALU.add)
                        if ch == nch - 1:
                            if isq:
                                inc(ins, "vm")
                            else:
                                inc(ins, "kbU")

            if not isq:
                @block.scalar
                def _(sc):
                    for ht in range(DTl):
                        sc.wait_ge(sems["kbU"], kbU0 + ht + 1)
                        inc(sc.dma_start(ktspill[ht * 128:(ht + 1) * 128, :], kbf),
                            "aq", 16)
                        sc.sem_inc(sems["vm"], 1)
                        cnt["vm"] += 1

        if upto >= 3:
            proj_xpos(False)
        if upto >= 4:
            proj_xpos(True)
            dbg_dump([(dbg["dbg_qt"], cb(bigb, 128 * KB, DTl * NT))] if DEBUG else [])

        if upto >= 5:
            gate()
            sq0, paU0, paF0 = cnt["sq"], cnt["paU"], cnt["paF"]

            @block.sync
            def _(s):
                for mt in range(16):
                    if mt >= 2:
                        s.wait_ge(sems["paF"], paF0 + mt - 1)
                    src = bass.AP(ktspill.tensor, ktspill.offset + mt * 128,
                                  [[S, 128], [128 * S, 16], [1, 128]])
                    inc(s.dma_start(kblk[mt % 2].rearrange("p (a b) -> p a b", a=16),
                                    src), "sq", 16)
                    inc(s.dma_start(dstage[mt % 2],
                                    dt_in[mt * 128:(mt + 1) * 128, :]), "sq", 16)

            @block.tensor
            def _(t):
                for mt in range(16):
                    t.wait_ge(sems["sq"], sq0 + 32 * (mt + 1))
                    if mt >= 2:
                        t.wait_ge(sems["paF"], paF0 + mt - 1)
                    sl = (mt % 2) * 1024
                    for ch2 in range(2):
                        for ht in range(DTl):
                            ins = t.matmul(pA[:, sl + ch2 * 512: sl + ch2 * 512 + 512],
                                           kblk[mt % 2][:, ht * 128:(ht + 1) * 128],
                                           qt_t(ht)[:, ch2 * 512:ch2 * 512 + 512],
                                           start=(ht == 0), stop=(ht == DTl - 1))
                    inc(ins, "paU")

            @block.vector
            def _(v):
                for mt in range(16):
                    v.wait_ge(sems["paU"], paU0 + mt + 1)
                    sl = (mt % 2) * 1024
                    inc(v.tensor_tensor(att_t(mt), pA[:, sl:sl + 1024],
                                        dstage[mt % 2], op=ALU.mult), "paF")

            dbg_dump([(dbg["dbg_att"], cb(bigb, 32 * KB, DTl * NT))] if DEBUG else [])

        if upto >= 6:
            gate()
            sq0, paU0, paF0, tm0 = cnt["sq"], cnt["paU"], cnt["paF"], cnt["tm"]

            @block.sync
            def _(s):
                for g in range(8):
                    if g >= 2:
                        s.wait_ge(sems["paU"], paU0 + g - 1)
                    src = bass.AP(vspill.tensor, vspill.offset + g * 256,
                                  [[D, 128], [128 * D, 16], [1, 256]])
                    inc(s.dma_start(vgblk[g % 2].rearrange("p (a b) -> p a b", a=16),
                                    src), "sq", 16)

            @block.tensor
            def _(t):
                for g in range(8):
                    t.wait_ge(sems["sq"], sq0 + 16 * (g + 1))
                    if g >= 1:
                        t.wait_ge(sems["paF"], paF0 + g)
                    for m in range(16):
                        for dd in range(2):
                            for ch2 in range(2):
                                ins = t.matmul(
                                    pA[:, (dd * 2 + ch2) * 512:(dd * 2 + ch2) * 512 + 512],
                                    vgblk[g % 2][:, m * 256 + dd * 128:m * 256 + dd * 128 + 128],
                                    att_t(m)[:, ch2 * 512:ch2 * 512 + 512],
                                    start=(m == 0), stop=(m == 15))
                    inc(ins, "paU")

            @block.vector
            def _(v):
                for g in range(8):
                    v.wait_ge(sems["paU"], paU0 + g + 1)
                    dst = sapf(96 * KB + (2 * g) * (NT * 4),
                               [[NT, 2], [512, 2], [1, 512]])
                    inc(v.tensor_copy(dst, pA.rearrange("p (a b c) -> p a b c",
                                                        a=2, b=2)), "paF")

            if upto == 6:
                dbg_dump([(dbg["dbg_oacc"], cf(bigf, 96 * KB, DTl * NT))]
                         if DEBUG else [])

        if upto >= 7:
            har_map = list(range(FT))

            def har_phys(pid):
                return har_t(pid) if pid < FT else bfly_sp[pid - FT]

            for ch in range(2):
                gate()
                har_map[:] = list(range(FT))
                c0 = ch * 512
                gq0, paU0, hbU0, evF0, sm0, wsF0, tmA0 = (
                    cnt["gq"], cnt["paU"], cnt["hbU"], cnt["evF"], cnt["sm"],
                    cnt["wsF"], cnt["tm"])

                @block.gpsimd
                def _(g):
                    for j1 in range(FT):
                        if j1 >= 2:
                            g.wait_ge(sems["paU"], paU0 + j1 - 1)
                        for wt, blk in ((w1t, w1blk), (w3t, w3blk)):
                            src = bass.AP(wt.tensor, wt.offset + j1 * 128,
                                          [[FF, 128], [128 * FF, 16], [1, 128]])
                            inc(g.dma_start(
                                blk[j1 % 2].rearrange("p (a b) -> p a b", a=16), src),
                                "gq", 16)

                def emit_a(t, k):
                    t.wait_ge(sems["hbU"], hbU0 + k + 1)
                    if k >= 3:
                        t.wait_ge(sems["evF"], evF0 + k - 2)
                    ins = t.matmul(pB[:, (k % 3) * 512:(k % 3) * 512 + 512],
                                   h128_b, hbr[k % 3], start=True, stop=True)
                    inc(ins, "tm")

                @block.tensor
                def _(t):
                    for j1 in range(FT):
                        t.wait_ge(sems["gq"], gq0 + 32 * (j1 + 1))
                        if j1 >= 2:
                            t.wait_ge(sems["sm"], sm0 + j1 - 1)
                            t.wait_ge(sems["hbU"], hbU0 + j1 - 1)
                        sl = (j1 % 2) * 1024
                        for k in range(DTl):
                            t.matmul(pA[:, sl:sl + 512],
                                     w1blk[j1 % 2][:, k * 128:(k + 1) * 128],
                                     xnT(0, k)[:, c0:c0 + 512],
                                     start=(k == 0), stop=(k == DTl - 1))
                        for k in range(DTl):
                            ins = t.matmul(pA[:, sl + 512:sl + 1024],
                                           w3blk[j1 % 2][:, k * 128:(k + 1) * 128],
                                           xnT(0, k)[:, c0:c0 + 512],
                                           start=(k == 0), stop=(k == DTl - 1))
                        inc(ins, "paU")
                        if j1 >= 2:
                            emit_a(t, j1 - 2)
                    emit_a(t, FT - 2)
                    emit_a(t, FT - 1)

                @block.scalar
                def _(sc):
                    for j1 in range(FT):
                        sc.wait_ge(sems["paU"], paU0 + j1 + 1)
                        if j1 >= 2:
                            sc.wait_ge(sems["hbU"], hbU0 + j1 - 1)
                        sl = (j1 % 2) * 1024
                        inc(sc.activation(stmp[j1 % 2], pA[:, sl:sl + 512], AF.Silu),
                            "sm")

                @block.vector
                def _(v):
                    def emit_ev(v, k):
                        v.wait_ge(sems["tm"], tmA0 + k + 1)
                        ins = v.tensor_copy(har_t(k),
                                            pB[:, (k % 3) * 512:(k % 3) * 512 + 512])
                        inc(ins, "evF")
                    for j1 in range(FT):
                        v.wait_ge(sems["sm"], sm0 + j1 + 1)
                        if j1 >= 3:
                            v.wait_ge(sems["tm"], tmA0 + j1 - 2)
                        sl = (j1 % 2) * 1024
                        ins = v.scalar_tensor_tensor(hbr[j1 % 3], stmp[j1 % 2],
                                                     INV_SQRT_FF,
                                                     pA[:, sl + 512:sl + 1024],
                                                     op0=ALU.mult, op1=ALU.mult)
                        inc(ins, "hbU")
                        if j1 >= 2:
                            emit_ev(v, j1 - 2)
                    emit_ev(v, FT - 2)
                    emit_ev(v, FT - 1)

                @block.vector
                def _(v):
                    v.wait_ge(sems["evF"], cnt["evF"])
                    free = [FT, FT + 1, FT + 2, FT + 3]
                    for st in range(BFLY_STAGES):
                        sig = 1 << st
                        pairs = [(j, j + sig) for b0 in range(0, FT, 2 * sig)
                                 for j in range(b0, b0 + sig)]
                        for (a, b) in pairs:
                            pa_, pb_ = har_map[a], har_map[b]
                            d1, d2 = free[0], free[1]
                            v.tensor_tensor(har_phys(d1), har_phys(pa_),
                                            har_phys(pb_), op=ALU.add)
                            v.tensor_tensor(har_phys(d2), har_phys(pa_),
                                            har_phys(pb_), op=ALU.subtract)
                            free = free[2:] + [pa_, pb_]
                            har_map[a], har_map[b] = d1, d2
                        v.drain()
                    arena_free = [p for p in free if p < FT]
                    for a in range(FT):
                        if har_map[a] >= FT:
                            dstp = arena_free.pop()
                            v.tensor_copy(har_phys(dstp), har_phys(har_map[a]))
                            har_map[a] = dstp
                    inc(v.drain(), "vm")
                vm_bfly = cnt["vm"]
                HAR_MAPS.append(list(har_map))

                gq1, paU1, tmW0, paFW = cnt["gq"], cnt["paU"], cnt["tm"], cnt["paF"]

                def _w2sig(gi):
                    return (gi // 8) * 7 + min(gi % 8 + 1, 7)

                @block.gpsimd
                def _(g):
                    g.wait_ge(sems["vm"], vm_bfly)
                    for ggrp in range(4):
                        for j8 in range(8):
                            gi = ggrp * 8 + j8
                            if gi >= 2:
                                pgi = gi - 2
                                if pgi % 8 == 7:
                                    g.wait_ge(sems["paU"], paU1 + pgi // 8 + 1)
                                else:
                                    g.wait_ge(sems["tm"], tmW0 + _w2sig(pgi))
                            src = bass.AP(w2t.tensor,
                                          w2t.offset + (j8 * 1024) * D + ggrp * 512,
                                          [[D, 128], [128 * D, 8], [1, 512]])
                            inc(g.dma_start(
                                w2blk[gi % 2].rearrange("p (a b) -> p a b", a=8),
                                src), "gq", 16)

                @block.tensor
                def _(t):
                    t.wait_ge(sems["vm"], vm_bfly)
                    for ggrp in range(4):
                        if ggrp >= 1:
                            t.wait_ge(sems["paF"], paFW + ggrp)
                        for j8 in range(8):
                            gi = ggrp * 8 + j8
                            t.wait_ge(sems["gq"], gq1 + 16 * (gi + 1))
                            for js in range(8):
                                j = j8 * 8 + js
                                for dd in range(4):
                                    ins = t.matmul(
                                        pA[:, dd * 512:(dd + 1) * 512],
                                        w2blk[gi % 2][:, js * 512 + dd * 128:js * 512 + dd * 128 + 128],
                                        har_phys(har_map[j]),
                                        start=(j == 0), stop=(j == FT - 1))
                            if j8 < 7:
                                inc(ins, "tm")
                            else:
                                inc(ins, "paU")

                @block.vector
                def _(v):
                    for ggrp in range(4):
                        v.wait_ge(sems["paU"], paU1 + ggrp + 1)
                        dst = sapf(96 * KB + (4 * ggrp) * (NT * 4) + c0 * 4,
                                   [[NT, 4], [1, 512]])
                        ins = v.tensor_tensor(dst, dst,
                                              pA.rearrange("p (a b) -> p a b", a=4),
                                              op=ALU.add)
                        inc(ins, "paF")

            dbg_dump([(dbg["dbg_oacc"], cf(bigf, 96 * KB, DTl * NT)),
                      (dbg["dbg_hid"], cb(bigb, 32 * KB, FT * 512))] if DEBUG else [])

        if upto >= 8:
            gate()
            sq0, paU0, paF0, vm0, aq0 = (cnt["sq"], cnt["paU"], cnt["paF"],
                                         cnt["vm"], cnt["aq"])

            @block.sync
            def _(s):
                for i in range(8):
                    if i >= 2:
                        s.wait_ge(sems["vm"], vm0 + (i - 1))
                    inc(s.dma_start(xstage[i % 2], x_own[i * 128:(i + 1) * 128, :]),
                        "sq", 16)

            @block.tensor
            def _(t):
                for i in range(8):
                    if i >= 1:
                        t.wait_ge(sems["paF"], paF0 + i)
                    for j in range(DTl):
                        ins = t.transpose(pA[:, j * 128:(j + 1) * 128],
                                          oacc_t(j)[:, i * 128:(i + 1) * 128],
                                          ident_f)
                    inc(ins, "paU")

            @block.vector
            def _(v):
                for i in range(8):
                    v.wait_ge(sems["paU"], paU0 + i + 1)
                    v.wait_ge(sems["sq"], sq0 + 16 * (i + 1))
                    if i >= 2:
                        v.wait_ge(sems["aq"], aq0 + 16 * (i - 1))
                    inc(v.tensor_tensor(x1buf[i % 2], xstage[i % 2], pA[:, 0:2048],
                                        op=ALU.add), "vm")

            @block.scalar
            def _(sc):
                for i in range(8):
                    sc.wait_ge(sems["vm"], vm0 + i + 1)
                    inc(sc.dma_start(x1spill[i * 128:(i + 1) * 128, :], x1buf[i % 2]),
                        "aq", 16)
                    sc.sem_inc(sems["paF"], 1)
                    cnt["paF"] += 1

            gate()

            def load_x1(s, i, slot):
                inc(s.dma_start(xstage[slot], x1spill[i * 128:(i + 1) * 128, :]),
                    "sq", 16)

            def arena1(i):
                return (32 * KB, i * 128)

            ln_transpose(8, load_x1, arena1)

            dbg_dump([(dbg["dbg_xn1T"], cb(bigb, 32 * KB, DTl * NT))] if DEBUG else [])

        if upto >= 9:
            gate()

            @block.vector
            def _(v):
                for j in range(4):
                    inc(v.memset(cf(bigf, 96 * KB + j * 16 * KB, 4 * NT), 0.0), "vm")
            zero_vm = cnt["vm"]

            ptU0, ptF0, pbU0, paU0, paF0, sm0, vm0 = (
                cnt["ptU"], cnt["ptF"], cnt["pbU"], cnt["paU"], cnt["paF"],
                cnt["sm"], cnt["vm"])

            @block.tensor
            def _(t):
                it = 0
                for sg in range(2):
                    for hd in range(16):
                        par = hd % 2
                        gT = xn1T(hd)[:, sg * 512 + par: sg * 512 + par + 511:2]
                        if it >= 1:
                            t.wait_ge(sems["ptF"], ptF0 + it)
                        t.transpose(pT[:, 0:128], gT[:, 0:128], ident_b)
                        inc(t.transpose(pT[:, 128:256], gT[:, 128:256], ident_b),
                            "ptU")
                        if it >= 1:
                            t.wait_ge(sems["sm"], sm0 + 2 * it)
                        for q2 in range(2):
                            ins = t.matmul(pB[:, q2 * 256: q2 * 256 + 256],
                                           gT[:, q2 * 128:(q2 + 1) * 128], gT,
                                           start=True, stop=True)
                        inc(ins, "pbU")
                        t.wait_ge(sems["vm"], zero_vm + 3 * it + 2)
                        if it >= 1:
                            t.wait_ge(sems["vm"], zero_vm + 3 * it)
                        for q2 in range(2):
                            for k2 in range(2):
                                ins = t.transpose(
                                    pB_bf[:, 2048 + (k2 * 2 + q2) * 128:
                                          2048 + (k2 * 2 + q2) * 128 + 128],
                                    a_sb[:, q2 * 256 + k2 * 128:
                                         q2 * 256 + k2 * 128 + 128],
                                    ident_b)
                        inc(ins, "ptU")
                        t.wait_ge(sems["vm"], zero_vm + 3 * it + 3)
                        if it >= 1:
                            t.wait_ge(sems["paF"], paF0 + it)
                        for k2 in range(2):
                            ins = t.matmul(pA[:, 0:256],
                                           g_sb[:, k2 * 128:(k2 + 1) * 128],
                                           at_sb[:, k2 * 256:(k2 + 1) * 256],
                                           start=(k2 == 0), stop=(k2 == 1))
                        inc(ins, "paU")
                        it += 1

            @block.scalar
            def _(sc):
                for it in range(32):
                    sc.wait_ge(sems["vm"], zero_vm + 3 * it + 1)
                    for q2 in range(2):
                        inc(sc.activation(e_sb[:, q2 * 256:(q2 + 1) * 256],
                                          pB[:, q2 * 256: q2 * 256 + 256], AF.Exp,
                                          bias=nmax2[:, q2:q2 + 1], scale=L1_SCALE,
                                          accum_out=zden2[:, q2:q2 + 1]), "sm")
                    sc.drain()

            @block.vector
            def _(v):
                v.wait_ge(sems["vm"], zero_vm)
                it = 0
                for sg in range(2):
                    for hd in range(16):
                        par = hd % 2
                        v.wait_ge(sems["pbU"], pbU0 + it + 1)
                        for q2 in range(2):
                            v.reduce_max(nmax2[:, q2:q2 + 1],
                                         pB[:, q2 * 256: q2 * 256 + 256],
                                         axis=AX.X, negate=True)
                        v.drain()
                        inc(v.tensor_scalar(nmax2, nmax2, L1_SCALE, None,
                                            op0=ALU.mult), "vm")
                        v.wait_ge(sems["sm"], sm0 + 2 * it + 2)
                        v.reciprocal(rz2, zden2)
                        v.drain()
                        for q2 in range(2):
                            ins = v.tensor_scalar(a_sb[:, q2 * 256:(q2 + 1) * 256],
                                                  e_sb[:, q2 * 256:(q2 + 1) * 256],
                                                  rz2[:, q2:q2 + 1], None,
                                                  op0=ALU.mult)
                        inc(ins, "vm")
                        v.wait_ge(sems["ptU"], ptU0 + 2 * it + 1)
                        inc(v.tensor_copy(g_sb, pT[:, 0:256]), "ptF")
                        v.wait_ge(sems["ptU"], ptU0 + 2 * it + 2)
                        inc(v.tensor_copy(at_sb, pB_bf[:, 2048:2560]), "vm")
                        v.wait_ge(sems["paU"], paU0 + it + 1)
                        dst = sapf(96 * KB + hd * (NT * 4) + (sg * 512 + par) * 4,
                                   [[2, 256]])
                        inc(v.tensor_tensor(dst, dst, pA[:, 0:256], op=ALU.add),
                            "paF")
                        it += 1

            if upto == 9:
                dbg_dump([(dbg["dbg_out1"], cf(bigf, 96 * KB, DTl * NT))]
                         if DEBUG else [])

        if upto >= 10:
            gate()
            paU0, paF0, vm0, sm0, pbU0 = (cnt["paU"], cnt["paF"], cnt["vm"],
                                          cnt["sm"], cnt["pbU"])

            @block.tensor
            def _(t):
                for i in range(8):
                    if i >= 1:
                        t.wait_ge(sems["paF"], paF0 + i)
                    for k in range(DTl):
                        ins = t.matmul(pA[:, 0:16],
                                       xn1T(k)[:, i * 128:(i + 1) * 128],
                                       gw_blk[:, k * E:(k + 1) * E],
                                       start=(k == 0), stop=(k == DTl - 1))
                    inc(ins, "paU")
                t.wait_ge(sems["vm"], vm0 + 3 * 8)
                for i in range(8):
                    ins = t.transpose(pB[0:16, i * 128:(i + 1) * 128],
                                      cw_all[:, i * E:(i + 1) * E], ident_f)
                inc(ins, "pbU")

            @block.vector
            def _(v):
                for i in range(8):
                    v.wait_ge(sems["paU"], paU0 + i + 1)
                    inc(v.tensor_copy(gj[:, 0:16], pA[:, 0:16]), "paF")
                    v.drain()
                    inc(v.reduce_max(nmax2[:, 0:1], gj[:, 0:16], axis=AX.X,
                                     negate=True), "vm")
                    v.wait_ge(sems["sm"], sm0 + i + 1)
                    v.reciprocal(rz2[:, 0:1], zden2[:, 0:1])
                    v.drain()
                    inc(v.tensor_scalar(cw_all[:, i * E:(i + 1) * E],
                                        e_sb.bitcast(F32)[:, 0:16],
                                        rz2[:, 0:1], None, op0=ALU.mult), "vm")
                    v.drain()
                    v.max(m8t, cw_all[:, i * E:(i + 1) * E])
                    v.drain()
                    v.tensor_scalar(gj[:, 0:16], cw_all[:, i * E:(i + 1) * E],
                                    m8t[:, 5:6], None, op0=ALU.is_ge)
                    v.drain()
                    inc(v.tensor_tensor(cw_all[:, i * E:(i + 1) * E],
                                        cw_all[:, i * E:(i + 1) * E], gj[:, 0:16],
                                        op=ALU.mult), "vm")
                v.wait_ge(sems["pbU"], pbU0 + 1)
                inc(v.tensor_copy(cwT[0:16, :].rearrange("p (a b) -> p a b", a=8),
                                  pB[0:16, 0:1024].rearrange("p (a b) -> p a b",
                                                             a=8)), "pbF")

            @block.scalar
            def _(sc):
                for i in range(8):
                    sc.wait_ge(sems["vm"], vm0 + 3 * i + 1)
                    inc(sc.activation(e_sb.bitcast(F32)[:, 0:16], gj[:, 0:16],
                                      AF.Exp, bias=nmax2[:, 0:1], scale=1.0,
                                      accum_out=zden2[:, 0:1]), "sm")
                    sc.drain()

            dbg_dump([(dbg["dbg_cw"], cw_all)] if DEBUG else [])

        if upto >= 11:
            gate()

            @block.gpsimd
            def _(g):
                inc(g.dma_start(sel_b[0:16, :], sel_in[:, :]), "gq", 16)
            sel_gq = cnt["gq"]

            for pr in range(8):
                gq0, paU0, pbU0, pbF0 = (cnt["gq"], cnt["paU"], cnt["pbU"],
                                         cnt["pbF"])
                sm0, hbU0, wsF0, evF0, tm0 = (cnt["sm"], cnt["hbU"], cnt["wsF"],
                                              cnt["evF"], cnt["tm"])
                paF_pair = cnt["paF"]

                @block.tensor
                def _(t):
                    if pr == 0:
                        t.wait_ge(sems["pbF"], cnt["pbF"])
                        t.wait_ge(sems["gq"], sel_gq)
                    for ee in range(2):
                        e = 2 * pr + ee
                        if ee == 1:
                            t.wait_ge(sems["pbF"], pbF0 + 1)
                        for ch2 in range(2):
                            ins = t.matmul(pB[:, ch2 * 512:(ch2 + 1) * 512],
                                           sel_b[0:16, e * 128:(e + 1) * 128],
                                           cwT[0:16, ch2 * 512:(ch2 + 1) * 512],
                                           start=True, stop=True)
                        inc(ins, "pbU")

                @block.vector
                def _(v):
                    for ee in range(2):
                        v.wait_ge(sems["pbU"], pbU0 + ee + 1)
                        inc(v.tensor_copy(bcw[ee], pB[:, 0:1024]), "pbF")

                @block.gpsimd
                def _(g):
                    for ee in range(2):
                        e = 2 * pr + ee
                        for mo in range(MOT):
                            gi = ee * MOT + mo
                            if gi >= 2:
                                g.wait_ge(sems["paU"], paU0 + gi - 1)
                            for wt, blk in ((wgt, w1blk), (wut, w3blk)):
                                src = bass.AP(wt.tensor,
                                              wt.offset + e * D * MO + mo * 128,
                                              [[MO, 128], [128 * MO, 16], [1, 128]])
                                inc(g.dma_start(
                                    blk[gi % 2].rearrange("p (a b) -> p a b", a=16),
                                    src), "gq", 16)

                @block.tensor
                def _(t):
                    t.wait_ge(sems["paF"], paF_pair)
                    t.wait_ge(sems["pbF"], pbF0 + 2)
                    for ee in range(2):
                        for mo in range(MOT):
                            gi = ee * MOT + mo
                            t.wait_ge(sems["gq"], gq0 + 32 * (gi + 1))
                            if gi >= 1:
                                t.wait_ge(sems["sm"], sm0 + gi)
                            if gi >= 2:
                                t.wait_ge(sems["hbU"], hbU0 + gi - 1)
                            usl = (gi % 2) * 1024
                            for ch2 in range(2):
                                for k in range(DTl):
                                    ins = t.matmul(
                                        pB[:, ch2 * 512:(ch2 + 1) * 512],
                                        w1blk[gi % 2][:, k * 128:(k + 1) * 128],
                                        xn1T(k)[:, ch2 * 512:(ch2 + 1) * 512],
                                        start=(k == 0), stop=(k == DTl - 1))
                            inc(ins, "wsF")
                            for ch2 in range(2):
                                for k in range(DTl):
                                    ins = t.matmul(
                                        pA[:, usl + ch2 * 512:usl + ch2 * 512 + 512],
                                        w3blk[gi % 2][:, k * 128:(k + 1) * 128],
                                        xn1T(k)[:, ch2 * 512:(ch2 + 1) * 512],
                                        start=(k == 0), stop=(k == DTl - 1))
                            inc(ins, "paU")

                @block.scalar
                def _(sc):
                    for gi in range(2 * MOT):
                        sc.wait_ge(sems["wsF"], wsF0 + gi + 1)
                        if gi >= 2:
                            sc.wait_ge(sems["hbU"], hbU0 + gi - 1)
                        inc(sc.activation(stmp2[gi % 2], pB[:, 0:1024], AF.Silu),
                            "sm")

                @block.vector
                def _(v):
                    for ee in range(2):
                        for mo in range(MOT):
                            gi = ee * MOT + mo
                            v.wait_ge(sems["sm"], sm0 + gi + 1)
                            v.tensor_tensor(stmp2[gi % 2], stmp2[gi % 2], bcw[ee],
                                            op=ALU.mult)
                            v.drain()
                            v.wait_ge(sems["paU"], paU0 + gi + 1)
                            usl = (gi % 2) * 1024
                            inc(v.tensor_tensor(hidT_t(gi), stmp2[gi % 2],
                                                pA[:, usl:usl + 1024], op=ALU.mult),
                                "hbU")

                gq1, paU1 = cnt["gq"], cnt["paU"]
                paF_dn = cnt["paF"]

                def _dnsig(gi):
                    return (gi // 4) * 3 + min(gi % 4 + 1, 3)

                @block.gpsimd
                def _(g):
                    for dpair in range(8):
                        for ee in range(2):
                            for dd in range(2):
                                e = 2 * pr + ee
                                dti = dpair * 2 + dd
                                gi = dpair * 4 + ee * 2 + dd
                                if gi >= 2:
                                    pgi = gi - 2
                                    if pgi % 4 == 3:
                                        g.wait_ge(sems["paU"],
                                                  paU1 + pgi // 4 + 1)
                                    else:
                                        g.wait_ge(sems["evF"],
                                                  evF0 + _dnsig(pgi))
                                src = bass.AP(wdt.tensor,
                                              wdt.offset + e * MO * D + dti * 128,
                                              [[D, 128], [128 * D, MOT], [1, 128]])
                                inc(g.dma_start(
                                    wdblk[gi % 2].rearrange("p (a b) -> p a b",
                                                            a=MOT), src), "gq", 16)

                @block.tensor
                def _(t):
                    t.wait_ge(sems["hbU"], hbU0 + 2 * MOT)
                    for dpair in range(8):
                        if dpair >= 1:
                            t.wait_ge(sems["paF"], paF_dn + dpair)
                        for ee in range(2):
                            for dd in range(2):
                                gi = dpair * 4 + ee * 2 + dd
                                t.wait_ge(sems["gq"], gq1 + 16 * (gi + 1))
                                for mo in range(MOT):
                                    for ch2 in range(2):
                                        ins = t.matmul(
                                            pA[:, dd * 1024 + ch2 * 512:
                                               dd * 1024 + ch2 * 512 + 512],
                                            wdblk[gi % 2][:, mo * 128:(mo + 1) * 128],
                                            hidT_t(ee * MOT + mo)[:, ch2 * 512:(ch2 + 1) * 512],
                                            start=(ee == 0 and mo == 0),
                                            stop=(ee == 1 and mo == MOT - 1))
                                if gi % 4 == 3:
                                    inc(ins, "paU")
                                else:
                                    inc(ins, "evF")

                @block.vector
                def _(v):
                    for dpair in range(8):
                        v.wait_ge(sems["paU"], paU1 + dpair + 1)
                        dst = sapf(96 * KB + (2 * dpair) * (NT * 4),
                                   [[NT, 2], [1, 1024]])
                        inc(v.tensor_tensor(dst, dst,
                                            pA.rearrange("p (a b) -> p a b", a=2),
                                            op=ALU.add), "paF")

            dbg_dump([(dbg["dbg_out1"], cf(bigf, 96 * KB, DTl * NT))] if DEBUG else [])

        if upto >= 12:
            gate()
            sq0, paU0, paF0, vm0, aq0 = (cnt["sq"], cnt["paU"], cnt["paF"],
                                         cnt["vm"], cnt["aq"])

            @block.sync
            def _(s):
                for i in range(8):
                    if i >= 2:
                        s.wait_ge(sems["vm"], vm0 + (i - 1))
                    inc(s.dma_start(xstage[i % 2], x1spill[i * 128:(i + 1) * 128, :]),
                        "sq", 16)

            @block.tensor
            def _(t):
                for i in range(8):
                    if i >= 1:
                        t.wait_ge(sems["paF"], paF0 + i)
                    for j in range(DTl):
                        ins = t.transpose(pA[:, j * 128:(j + 1) * 128],
                                          out1_t(j)[:, i * 128:(i + 1) * 128],
                                          ident_f)
                    inc(ins, "paU")

            @block.vector
            def _(v):
                for i in range(8):
                    v.wait_ge(sems["paU"], paU0 + i + 1)
                    v.wait_ge(sems["sq"], sq0 + 16 * (i + 1))
                    if i >= 2:
                        v.wait_ge(sems["aq"], aq0 + 16 * (i - 1))
                    inc(v.tensor_tensor(ybuf[i % 2], xstage[i % 2], pA[:, 0:2048],
                                        op=ALU.add), "vm")

            @block.scalar
            def _(sc):
                for i in range(8):
                    sc.wait_ge(sems["vm"], vm0 + i + 1)
                    inc(sc.dma_start(y_out[i * 128:(i + 1) * 128, :], ybuf[i % 2]),
                        "aq", 16)
                    sc.sem_inc(sems["paF"], 1)
                    cnt["paF"] += 1
                sc.wait_ge(sems["aq"], cnt["aq"])

        @block.sync
        def _(s):
            s.wait_ge(sems["aq"], cnt["aq"])

    return nc


def make_in_maps(L0_attn_X, L0_attn_W_Q, L0_attn_W_K, L0_attn_W_V,
                 L0_ffn_W1, L0_ffn_W2, L0_ffn_W3,
                 L1_ffn_gate_w, L1_ffn_w_gate, L1_ffn_w_up, L1_ffn_w_down):
    X = np.ascontiguousarray(np.asarray(L0_attn_X, dtype=np.float32))
    wq = np.ascontiguousarray(np.asarray(L0_attn_W_Q, dtype=np.float32))
    wk = np.ascontiguousarray(np.asarray(L0_attn_W_K, dtype=np.float32))
    wv = np.ascontiguousarray(np.asarray(L0_attn_W_V, dtype=np.float32))
    w1t = np.ascontiguousarray(np.asarray(L0_ffn_W1, dtype=np.float32).T)
    w3t = np.ascontiguousarray(np.asarray(L0_ffn_W3, dtype=np.float32).T)
    w2t = np.ascontiguousarray(np.asarray(L0_ffn_W2, dtype=np.float32).T)
    gwt = np.ascontiguousarray(np.asarray(L1_ffn_gate_w, dtype=np.float32).T)
    wgt = np.ascontiguousarray(
        np.asarray(L1_ffn_w_gate, dtype=np.float32).transpose(0, 2, 1))
    wut = np.ascontiguousarray(
        np.asarray(L1_ffn_w_up, dtype=np.float32).transpose(0, 2, 1))
    wdt = np.ascontiguousarray(
        np.asarray(L1_ffn_w_down, dtype=np.float32).transpose(0, 2, 1))

    cosQ, sinQ = _xpos_tables(False)
    cosK, sinK = _xpos_tables(True)
    DmT = _decay_T()
    h128 = _hadamard(128)
    prot = _prot_T()
    ident = np.eye(128, dtype=np.float32)
    ones1 = np.ones((1, 128), dtype=np.float32)
    sel = np.zeros((16, D), dtype=np.float32)
    for e_ in range(16):
        sel[e_, e_ * 128:(e_ + 1) * 128] = 1.0

    in_maps = []
    for c in range(8):
        b, h = c // 2, c % 2
        own = slice(h * NT, (h + 1) * NT)
        perm = np.r_[h * NT:(h + 1) * NT, (1 - h) * NT:(2 - h) * NT]
        im = {
            "x_own": np.ascontiguousarray(X[b, own]),
            "x_oth": np.ascontiguousarray(X[b, slice((1 - h) * NT, (2 - h) * NT)]),
            "wq": wq, "wk": wk, "wv": wv,
            "w1t": w1t, "w3t": w3t, "w2t": w2t,
            "gwt": gwt, "wgt": wgt, "wut": wut, "wdt": wdt,
            "cosq": np.ascontiguousarray(cosQ[own].T),
            "sinq": np.ascontiguousarray(sinQ[own].T),
            "cosk": np.ascontiguousarray(cosK[perm].T),
            "sink": np.ascontiguousarray(sinK[perm].T),
            "dt": np.ascontiguousarray(DmT[perm][:, own]),
            "ident": ident, "h128": h128, "prot": prot, "ones1": ones1, "sel": sel,
        }
        in_maps.append(im)
    return in_maps


def kernel(L0_attn_X, L0_attn_W_Q, L0_attn_W_K, L0_attn_W_V,
           L0_ffn_W1, L0_ffn_W2, L0_ffn_W3,
           L1_ffn_gate_w, L1_ffn_w_gate, L1_ffn_w_up, L1_ffn_w_down):
    in_maps = make_in_maps(L0_attn_X, L0_attn_W_Q, L0_attn_W_K, L0_attn_W_V,
                           L0_ffn_W1, L0_ffn_W2, L0_ffn_W3,
                           L1_ffn_gate_w, L1_ffn_w_gate, L1_ffn_w_up, L1_ffn_w_down)
    nc = build_program()
    res = run_bass_kernel_spmd(nc, in_maps, core_ids=list(range(8)))
    out = np.empty((B, S, D), dtype=np.float32)
    for c in range(8):
        b, h = c // 2, c % 2
        out[b, h * NT:(h + 1) * NT] = res.results[c]["y"]
    return out
